# version 02 (final)
# baseline (speedup 1.0000x reference)
import numpy as np
import ml_dtypes
from contextlib import ExitStack

import concourse.bass as bass
import concourse.mybir as mybir
from concourse.bass_utils import run_bass_kernel_spmd

F32 = mybir.dt.float32
BF16 = mybir.dt.bfloat16
AF = mybir.ActivationFunctionType
ALU = mybir.AluOpType
AXX = mybir.AxisListType.X

T = 2048
D = 1024
NCH = 8
NH = 16
HS = 64
DFF = 4096
NFF = 32
TB = 512
NTB = 4
MAGIC = float(1.5 * 2 ** 23)
NCORES = 8


def _consts():
    ident = np.eye(128)
    masks = np.zeros((4, 128, TB), np.float32)
    for j in range(4):
        kg = 128 * j + np.arange(128)[:, None]
        masks[j] = np.where(kg > np.arange(TB)[None, :], -4000.0, 0.0)
    ind2 = np.zeros((128, 2), np.float32)
    ind2[0:64, 0] = 1.0
    ind2[64:128, 1] = 1.0
    return {
        "identb": ident.astype(ml_dtypes.bfloat16),
        "identf": ident.astype(np.float32),
        "ones_col_b": np.ones((128, 1), ml_dtypes.bfloat16),
        "ones_row_b": np.ones((1, 128), ml_dtypes.bfloat16),
        "ones_row_f": np.ones((1, 128), np.float32),
        "ind2": ind2.astype(ml_dtypes.bfloat16),
        "indt": np.ascontiguousarray(ind2.T).astype(ml_dtypes.bfloat16),
        "magicc": np.full((128, 1), 1.5 * 2 ** 23, np.float32),
        "ones_tb_b": np.ones((1, TB), ml_dtypes.bfloat16),
        "epsc": np.array([[1e-5, 1e-35, 1e-8, 0.0],
                          [1e-5, 1e-35, 1e-8, 0.0]], np.float32),
        "maskc": np.ascontiguousarray(
            masks.transpose(1, 0, 2).reshape(128, 4 * TB)
        ).astype(ml_dtypes.bfloat16),
    }


class MK:

    def __init__(self, race_dev=False, stop_after=None):
        self.race_dev = race_dev
        self.stop_after = stop_after
        self.nc = bass.Bass(
            "TRN2", target_bir_lowering=False, debug=False,
            detect_race_conditions=race_dev,
            dynamic_dma_scratch_size=2048,
        )
        self.es = ExitStack()
        self.cnt = {}
        self.sems = {}
        self.devcnt = {}

    def sem(self, name):
        if name not in self.sems:
            self.sems[name] = self.es.enter_context(self.nc.semaphore(name))
            self.cnt[name] = 0
        return self.sems[name]

    def inc(self, ins, name, n=1):
        ins.then_inc(self.sem(name), n)
        self.cnt[name] += n
        if ins is getattr(self, "_sec_last", None):
            self._sec_last_inc = (name, self.cnt[name])
        return self.cnt[name]

    def sig(self, eng, name, n=1):
        ins = getattr(self, "_sec_last", None)
        if ins is not None and not self.race_dev:
            if self._sec_last_inc is None:
                return self.inc(ins, name, n)
            lname, ltok = self._sec_last_inc
            eng.wait_ge(self.sems[lname], ltok)
        return self.inc(eng.nop(), name, n)

    def tok(self, name):
        self.sem(name)
        return self.cnt[name]

    def wt(self, eng, name, tokv=None):
        self.sem(name)
        v = self.cnt[name] if tokv is None else tokv
        if v > 0:
            eng.wait_ge(self.sems[name], v)

    def dv(self, eng, tag, ins):
        self._sec_last = ins
        self._sec_last_inc = None
        if self.race_dev or tag in ("v", "s", "g"):
            s = self.sem("dev_" + tag)
            self.devcnt.setdefault(tag, 0)
            ins.then_inc(s, 1)
            self.devcnt[tag] += 1
            eng.wait_ge(s, self.devcnt[tag])
            self._sec_last_inc = ("dev_" + tag, self.devcnt[tag])
        return ins

    def _sec(self):
        self._sec_last = None
        self._sec_last_inc = None

    def ET(self, fn):
        self._sec()
        self.block.tensor(fn)

    def EV(self, fn):
        self._sec()
        self.block.vector(fn)

    def ES(self, fn):
        self._sec()
        self.block.scalar(fn)

    def EG(self, fn):
        self._sec()
        self.block.gpsimd(fn)

    def EY(self, fn):
        self._sec()
        self.block.sync(fn)

    def sb(self, name, shape, dt, stack=None):
        return (stack or self.es).enter_context(
            self.nc.sbuf_tensor(name, shape, dt))

    def ps(self, name, shape, dt):
        return self.es.enter_context(self.nc.psum_tensor(name, shape, dt))

    def fence(self, engines="tvsgy"):
        E = {"t": self.ET, "v": self.EV, "s": self.ES,
             "g": self.EG, "y": self.EY}
        toks = {}
        for e in engines:
            def prod(eng, e=e):
                for name, c in list(self.cnt.items()):
                    if c <= 0 or name.startswith(("fn_", "dev_")):
                        continue
                    if name.startswith("d"):
                        if e == "y":
                            self.wt(eng, name)
                    elif name[0] == e and name[1] == "2":
                        self.wt(eng, name)
                self.inc(eng.nop(), "fn_" + e)
            E[e](prod)
            toks[e] = self.tok("fn_" + e)
        for e in engines:
            def cons(eng, e=e):
                for o in engines:
                    if o != e:
                        self.wt(eng, "fn_" + o, toks[o])
            E[e](cons)

    def build(self):
        nc = self.nc
        self.x_in = nc.dram_tensor("x", [D, T], F32, kind="ExternalInput").ap()
        self.out = nc.dram_tensor("out", [D, T], F32,
                                  kind="ExternalOutput").ap()
        self.dbg = nc.dram_tensor("dbg", [1, 256], F32,
                                  kind="ExternalOutput").ap()
        wnames = [
            ("L0_attn_w_kv_down", [D, 128]), ("L0_attn_w_kv_up", [128, 2048]),
            ("L0_attn_w_q", [D, D]), ("L0_attn_w_out", [D, D]),
            ("L0_ffn_W_in", [DFF, D]), ("L0_ffn_b_in", [DFF]),
            ("L0_ffn_W_out", [DFF, D]), ("L0_ffn_b_out", [D]),
            ("L1_attn_Wq", [D, D]), ("L1_attn_bq", [D]),
            ("L1_attn_Wk", [D, D]), ("L1_attn_bk", [D]),
            ("L1_attn_Wv", [D, D]), ("L1_attn_bv", [D]),
            ("L1_attn_Wo", [D, D]), ("L1_attn_bo", [D]),
            ("L1_attn_g", [NH]), ("L1_attn_Wg1", [D, 512]),
            ("L1_attn_bg1", [512]), ("L1_attn_Wg2", [512, NH]),
            ("L1_attn_bg2", [NH]),
            ("L1_ffn_w_up", [DFF, D]), ("L1_ffn_w_down", [D, DFF]),
        ]
        self.w = {n: nc.dram_tensor(n, s, F32, kind="ExternalInput").ap()
                  for n, s in wnames}
        cs = _consts()
        self.cdram = {k: nc.inline_tensor(v, "c_" + k) for k, v in cs.items()}
        self.hid = nc.dram_tensor("hid", [DFF, T], BF16, kind="Internal").ap()
        self.hq = nc.dram_tensor("hq", [DFF, T], BF16, kind="Internal").ap()
        self.wq2d = nc.dram_tensor("wq2d", [NFF, 128, D], BF16,
                                   kind="Internal").ap()
        self.wutd = nc.dram_tensor("wutd", [NFF, 128, 8 * 128], BF16,
                                   kind="Internal").ap()
        self.wdtd = nc.dram_tensor("wdtd", [NCH, 128, NFF * 128], BF16,
                                   kind="Internal").ap()
        self.b1d = nc.dram_tensor("b1d", [1, 4096], F32,
                                  kind="Internal").ap()
        self.b2d = nc.dram_tensor("b2d", [1, 8], F32, kind="Internal").ap()
        self.cc_in = nc.dram_tensor("cc_in", [1, 128], F32,
                                    kind="Internal").ap()
        self.cc_out = nc.dram_tensor("cc_out", [1, 128], F32,
                                     kind="Internal").ap()

        es = self.es
        with es:
            self.X32 = self.sb("X32", [128, NCH * T], F32)
            self.XB = self.sb("XB", [128, NCH * T], BF16)
            self.TMP32 = self.sb("TMP32", [128, T], F32)
            self.CONST = {}
            for k, v in cs.items():
                dt = BF16 if v.dtype == ml_dtypes.bfloat16 else F32
                self.CONST[k] = self.sb("K_" + k, list(v.shape), dt)
            self.DBGROW = self.sb("DBGROW", [1, 256], F32)
            self.MROW = self.sb("MROW", [1, TB], F32)
            self.AROW = self.sb("AROW", [1, TB], F32)
            self.BROW = self.sb("BROW", [1, TB], F32)
            self.SREC = self.MROW
            self.RB16 = self.sb("RB16", [1, TB], BF16)
            self.X2S = self.sb("X2S", [128, 1024], BF16)
            self.AM8 = self.sb("AM8", [128, 8], F32)
            self.AMX = self.sb("AMX", [1, 8], F32)
            self.SC = self.sb("SC", [128, 2], F32)
            self.ROWX = self.sb("ROWX", [1, 128], F32)
            self.CCS = self.sb("CCS", [1, 128], F32)
            self.G2 = self.sb("G2", [NH, 2], F32)
            self.GT = self.sb("GT", [NH, 1], BF16)
            self.GROW = self.sb("GROW", [1, NH], BF16)
            self.GBS = self.sb("GBS", [128, NH], F32)
            self.BCSB = self.sb("BCSB", [128, TB], F32)
            self.HLC = self.sb("HLC", [128, 2], BF16)
            self.HLR = self.sb("HLR", [1, 384], BF16)
            self.ROWY = self.sb("ROWY", [1, 128], F32)
            self.COL32 = self.sb("COL32", [128, 1], F32)
            self.HLROW = self.sb("HLROW", [1, 2048], BF16)
            self.BIN = self.sb("BIN", [128, NFF], F32)
            self.BOUT = self.sb("BOUT", [128, NCH], F32)
            self.BQ = self.sb("BQ", [128, NCH], F32)
            self.BK = self.sb("BK", [128, NCH], F32)
            self.BV = self.sb("BV", [128, NCH], F32)
            self.BO = self.sb("BO", [128, NCH], F32)
            self.BG1 = self.sb("BG1", [128, 4], F32)
            self.BG2 = self.sb("BG2", [NH, 1], F32)
            self.GIN = self.sb("GIN", [NH, 1], F32)
            self.THR = self.sb("THR", [128, 4], F32)
            self.PS0 = self.ps("PS0", [128, 1024], F32)
            self.PS1 = self.ps("PS1", [128, 1024], F32)
            self.PS4 = self.ps("PS4", [128, 1024], F32)
            self.PS2 = self.ps("PS2", [128, 512], F32)
            self.PSB = self.ps("PSB", [128, 1024], BF16)

            self.block = es.enter_context(nc.Block())
            self.p0_load()
            order = ["p1", "p2", "p3", "p4"]
            fns = {"p1": self.p1_l0_attn, "p2": self.p2_l0_ffn,
                   "p3": self.p3_l1_attn, "p4": self.p4_l1_ffn}
            for p in order:
                if self.stop_after is not None and p > self.stop_after:
                    break
                fns[p]()
            self.p5_store()
        return nc

    def p0_load(self):
        cd, C = self.cdram, self.CONST

        def z(v):
            self.dv(v, "v", v.memset(self.DBGROW[:, :], 0.0))
        self.EV(z)
        def loads(y):
            for k in C:
                self.inc(y.dma_start(out=C[k][:, :], in_=cd[k].ap()[:, :]),
                         "dC", 16)
        self.EY(loads)
        nconst16 = self.tok("dC")
        self.ET(lambda t: self.wt(t, "dC", nconst16))
        self.EV(lambda v: self.wt(v, "dC", nconst16))
        self.ES(lambda s: self.wt(s, "dC", nconst16))
        self.EG(lambda g: self.wt(g, "dC", nconst16))

        def ldx(y):
            for c in range(NCH):
                self.inc(y.dma_start(
                    out=self.X32[:, c * T:(c + 1) * T],
                    in_=self.x_in[c * 128:(c + 1) * 128, :]),
                    "dX0", 16)
        self.EY(ldx)
        dtok = self.tok("dX0")

        def mirror(v, dtok=dtok):
            self.wt(v, "dX0", dtok)
            for c in range(NCH):
                self.dv(v, "v", v.tensor_copy(
                    self.XB[:, c * T:(c + 1) * T],
                    self.X32[:, c * T:(c + 1) * T]))
            self.sig(v, "v2t")
        self.EV(mirror)



    SLOTS = [("PS0", 0), ("PS0", 512), ("PS1", 0), ("PS1", 512)]
    SLOTS2 = [("PS0", 0), ("PS0", 512), ("PS4", 0), ("PS4", 512)]

    def slot(self, g):
        name, off = self.SLOTS[g % 4]
        ps = getattr(self, name)
        return ps[:, off:off + 512]

    def slot2(self, g):
        name, off = self.SLOTS2[g % 4]
        ps = getattr(self, name)
        return ps[:, off:off + 512]

    def layernorm(self, tag, mirror=True):
        C = self.CONST
        def cast_pre(v):
            for c in range(NCH):
                self.dv(v, "v", v.tensor_copy(
                    self.XB[:, c * T:(c + 1) * T],
                    self.X32[:, c * T:(c + 1) * T]))
            self.sig(v, "v2t")
        self.EV(cast_pre)
        pre_tok = self.tok("v2t")

        x2tok = {}
        for tb in range(NTB):
            def mm_sum(t, tb=tb):
                self.wt(t, "v2t")
                for c in range(NCH):
                    rhs = self.XB[:, c * T + tb * TB: c * T + (tb + 1) * TB]
                    self.dv(t, "t", t.matmul(
                        self.PS4[0:1, 0:512], C["ones_col_b"][:, :], rhs,
                        start=(c == 0), stop=(c == NCH - 1)))
            self.ET(mm_sum)
            for c in range(NCH):
                sl = c % 2
                def x2(v, c=c, tb=tb, sl=sl):
                    if sl in x2tok:
                        self.wt(v, "t2v", x2tok[sl])
                    src = self.XB[:, c * T + tb * TB: c * T + (tb + 1) * TB]
                    dst = self.X2S[:, sl * TB:(sl + 1) * TB]
                    self.dv(v, "v", v.tensor_mul(dst, src, src))
                    self.sig(v, "v2t")
                self.EV(x2)
                vt = self.tok("v2t")
                def mm_sq(t, c=c, tb=tb, sl=sl, vt=vt):
                    self.wt(t, "v2t", vt)
                    self.dv(t, "t", t.matmul(
                        self.PS4[0:1, 512:1024], C["ones_col_b"][:, :],
                        self.X2S[:, sl * TB:(sl + 1) * TB],
                        start=(c == 0), stop=(c == NCH - 1)))
                    x2tok[sl] = self.sig(t, "t2v")
                self.ET(mm_sq)
            def mmdone(t):
                self.sig(t, "t2s")
            self.ET(mmdone)
            st_tok = self.tok("t2s")

            def rows_s(s, tb=tb, st_tok=st_tok):
                self.wt(s, "t2s", st_tok)
                self.dv(s, "s", s.activation(
                    self.MROW[:, :], self.PS4[0:1, 0:512], AF.Copy,
                    scale=1.0 / D))
                self.dv(s, "s", s.activation(
                    self.AROW[:, :], self.PS4[0:1, 512:1024], AF.Copy,
                    scale=1.0 / D))
                self.dv(s, "s", s.activation(
                    self.BROW[:, :], self.MROW[:, :], AF.Square))
                self.sig(s, "s2v")
            self.ES(rows_s)
            sv = self.tok("s2v")
            def rows_v(v, sv=sv):
                self.wt(v, "s2v", sv)
                self.dv(v, "v", v.tensor_sub(
                    self.AROW[:, :], self.AROW[:, :], self.BROW[:, :]))
                self.sig(v, "v2s")
            self.EV(rows_v)
            vs = self.tok("v2s")
            def rows_s2(s, vs=vs):
                self.wt(s, "v2s", vs)
                self.dv(s, "s", s.activation(
                    self.AROW[:, :], self.AROW[:, :], AF.Ln,
                    bias=self.CONST["epsc"][0:1, 0:1]))
                self.dv(s, "s", s.activation(
                    self.AROW[:, :], self.AROW[:, :], AF.Exp, scale=-0.5))
                self.sig(s, "s2v")
            self.ES(rows_s2)
            sv2 = self.tok("s2v")
            def rows_v2(v, sv2=sv2):
                self.wt(v, "s2v", sv2)
                self.dv(v, "v", v.scalar_tensor_tensor(
                    self.BROW[:, :], self.MROW[:, :], -1.0, self.AROW[:, :],
                    ALU.mult, ALU.mult))
                self.sig(v, "v2t")
            self.EV(rows_v2)
            rv = self.tok("v2t")

            def hl(v, rv=rv):
                self.wt(v, "s2v")
                for (row, off) in ((self.AROW, 0), (self.BROW, 1024)):
                    with self.nc.allow_low_precision("hi/lo split"):
                        self.dv(v, "v", v.tensor_copy(
                            self.HLROW[0:1, off:off + 512], row[:, :]))
                        self.dv(v, "v", v.tensor_sub(
                            self.TMP32[0:1, 0:512], row[:, :],
                            self.HLROW[0:1, off:off + 512]))
                        self.dv(v, "v", v.tensor_copy(
                            self.HLROW[0:1, off + 512:off + 1024],
                            self.TMP32[0:1, 0:512]))
                self.sig(v, "v2t")
            self.EV(hl)
            rv2 = self.tok("v2t")
            def bc(t, rv2=rv2):
                self.wt(t, "v2t", rv2)
                for (dst, off) in ((self.PS4[:, 0:512], 0),
                                   (self.PS4[:, 512:1024], 1024)):
                    self.dv(t, "t", t.matmul(
                        dst, C["ones_row_b"][0:1, :],
                        self.HLROW[0:1, off:off + 512],
                        start=True, stop=False))
                    self.dv(t, "t", t.matmul(
                        dst, C["ones_row_b"][0:1, :],
                        self.HLROW[0:1, off + 512:off + 1024],
                        start=False, stop=True))
                self.sig(t, "t2v")
            self.ET(bc)
            btok = self.tok("t2v")

            def apply(v, tb=tb, btok=btok):
                self.wt(v, "t2v", btok)
                for c in range(NCH):
                    xs = self.X32[:, c * T + tb * TB: c * T + (tb + 1) * TB]
                    self.dv(v, "v", v.scalar_tensor_tensor(
                        self.TMP32[:, 0:512], xs, 0.0, self.PS4[:, 0:512],
                        ALU.bypass, ALU.mult))
                    self.dv(v, "v", v.tensor_add(
                        xs, self.TMP32[:, 0:512], self.PS4[:, 512:1024]))
                    if mirror:
                        self.dv(v, "v", v.tensor_copy(
                            self.XB[:, c * T + tb * TB: c * T + (tb + 1) * TB],
                            xs))
                self.sig(v, "v2t")
            self.EV(apply)

    def p1_l0_attn(self):
        C = self.CONST
        st = ExitStack()
        with st:
            WKVD = self.sb("WKVD", [128, 8 * 128], BF16, st)
            WKVU = self.sb("WKVU", [128, 2048], BF16, st)
            WBIG = self.sb("WBIG", [128, 8192], BF16, st)
            LAT = self.sb("LAT", [128, T], BF16, st)
            QB = self.sb("QBp1", [128, T], BF16, st)
            KB = self.sb("KBp1", [128, T], BF16, st)
            VTMP = self.sb("VTMPp1", [128, T], BF16, st)
            VT = self.sb("VTp1", [128, 2 * 16 * 65], BF16, st)
            ATT = self.sb("ATTp1", [128, 16 * TB], BF16, st)
            YB = self.sb("YBp1", [128, NCH * T], BF16, st)

            def ones_cols(v):
                ap = VT[:, :].rearrange("p (n c) -> p n c", c=65)[:, :, 64:65]
                self.dv(v, "v", v.memset(ap, 1.0))
                self.sig(v, "v2t")
            self.EV(ones_cols)

            def ld_kvd(y):
                self.wt(y, "t2y")
                self.inc(y.dma_start(
                    out=self.TMP32[:, 0:1024].rearrange(
                        "p (d f) -> p d f", d=8),
                    in_=self.w["L0_attn_w_kv_down"][:, :].rearrange(
                        "(d p) f -> p d f", p=128)), "dW", 16)
            self.EY(ld_kvd)
            t_kvd = self.tok("dW")
            def cast_kvd(v, t_kvd=t_kvd):
                self.wt(v, "dW", t_kvd)
                self.dv(v, "v", v.tensor_copy(WKVD[:, :], self.TMP32[:, 0:1024]))
                self.sig(v, "v2y")
            self.EV(cast_kvd)
            kvd_done = self.tok("v2y")

            def ld_kvu(y, kvd_done=kvd_done):
                self.wt(y, "v2y", kvd_done)
                self.inc(y.dma_start(out=self.TMP32[:, :],
                                     in_=self.w["L0_attn_w_kv_up"][:, :]),
                         "dW", 16)
            self.EY(ld_kvu)
            t_kvu = self.tok("dW")
            def cast_kvu(v, t_kvu=t_kvu):
                self.wt(v, "dW", t_kvu)
                self.dv(v, "v", v.tensor_copy(WKVU[:, :], self.TMP32[:, :]))
                self.sig(v, "v2y")
            self.EV(cast_kvu)
            kvu_done = self.tok("v2y")

            wq_cast_tok = None
            for d in range(NCH):
                half = (d % 2) * 1024
                def ld_wq(y, d=d, half=half):
                    if d == 0:
                        self.wt(y, "v2y", kvu_done)
                    elif d >= 2:
                        self.wt(y, "v2y", self._wqtok[d - 2])
                    self.inc(y.dma_start(
                        out=self.TMP32[:, half:half + 1024],
                        in_=self.w["L0_attn_w_q"][d * 128:(d + 1) * 128, :]),
                        "dW" + str(d % 2), 16)
                self.EY(ld_wq)
                dt_ = self.tok("dW" + str(d % 2))
                def cast_wq(v, d=d, half=half, dt_=dt_):
                    self.wt(v, "dW" + str(d % 2), dt_)
                    self.dv(v, "v", v.tensor_copy(
                        WBIG[:, d * 1024:(d + 1) * 1024],
                        self.TMP32[:, half:half + 1024]))
                    self._wqtok = getattr(self, "_wqtok", {})
                    self._wqtok[d] = self.sig(v, "v2y")
                self.EV(cast_wq)
            wq_cast_tok = self.tok("v2y")

            evtok = {}
            for tb in range(NTB):
                g = tb
                def mm_lat(t, tb=tb, g=g):
                    if tb == 0:
                        self.wt(t, "v2y", kvd_done)
                        self.wt(t, "v2t")
                    for d in range(NCH):
                        self.dv(t, "t", t.matmul(
                            self.slot(g), WKVD[:, d * 128:(d + 1) * 128],
                            self.XB[:, d * T + tb * TB: d * T + (tb + 1) * TB],
                            start=(d == 0), stop=(d == NCH - 1)))
                    self.sig(t, "t2s")
                self.ET(mm_lat)
                mt = self.tok("t2s")
                def ev_lat(s, tb=tb, g=g, mt=mt):
                    self.wt(s, "t2s", mt)
                    self.dv(s, "s", s.copy(
                        LAT[:, tb * TB:(tb + 1) * TB], self.slot(g)))
                    evtok[g] = self.sig(s, "s2t")
                self.ES(ev_lat)

            last_exp = {}
            last_evac_y = {}
            last_bcast_use = None
            yi = 0
            gslot = NTB
            att_last_av = {}
            for c in range(NCH):
                for (dst, wofs, nacc, which) in (
                        (KB, c * 128, 1, "kv"), (VTMP, 1024 + c * 128, 1, "kv"),
                        (QB, c * 128, NCH, "q")):
                    for tb in range(NTB):
                        g = gslot; gslot += 1
                        def mmp(t, dst=dst, wofs=wofs, nacc=nacc, which=which,
                                tb=tb, g=g):
                            if g >= 4:
                                self.wt(t, "s2t", evtok.get(g - 4, 0) or 0)
                            if which == "q" and tb == 0 and c == 0:
                                self.wt(t, "v2y", wq_cast_tok)
                            if which == "kv" and tb == 0 and c == 0:
                                self.wt(t, "v2y", kvu_done)
                            if which == "kv":
                                self.dv(t, "t", t.matmul(
                                    self.slot(g), WKVU[:, wofs:wofs + 128],
                                    LAT[:, tb * TB:(tb + 1) * TB],
                                    start=True, stop=True))
                            else:
                                for d in range(NCH):
                                    self.dv(t, "t", t.matmul(
                                        self.slot(g),
                                        WBIG[:, d * 1024 + wofs:
                                             d * 1024 + wofs + 128],
                                        self.XB[:, d * T + tb * TB:
                                                d * T + (tb + 1) * TB],
                                        start=(d == 0), stop=(d == NCH - 1)))
                            self.sig(t, "t2s")
                        self.ET(mmp)
                        mt = self.tok("t2s")
                        def evp(s, dst=dst, tb=tb, g=g, mt=mt):
                            self.wt(s, "t2s", mt)
                            self.dv(s, "s", s.copy(
                                dst[:, tb * TB:(tb + 1) * TB], self.slot(g)))
                            evtok[g] = self.sig(s, "s2t")
                        self.ES(evp)

                vt_evac = {}
                for rnd in range(2):
                    def vtr(t, c=c, rnd=rnd):
                        self.wt(t, "s2t")
                        if rnd == 1 or c > 0:
                            self.wt(t, "v2t", vt_evac.get(rnd - 1,
                                    self._vt_prev if rnd == 0 else 0))
                        for j in range(8):
                            kc = rnd * 8 + j
                            self.dv(t, "t", t.transpose(
                                self.PSB[:, j * 128:(j + 1) * 128],
                                VTMP[:, kc * 128:(kc + 1) * 128],
                                C["identb"][:, :]))
                        self.sig(t, "t2v")
                    self.ET(vtr)
                    tt = self.tok("t2v")
                    def vte(v, rnd=rnd, tt=tt):
                        self.wt(v, "t2v", tt)
                        for hh in range(2):
                            src = self.PSB[:, :].rearrange(
                                "p (j k) -> p j k", j=8)[:, :,
                                hh * 64:hh * 64 + 64]
                            dst = VT[:, :].rearrange(
                                "p (n k) -> p n k", k=65)[
                                :, hh * 16 + rnd * 8: hh * 16 + rnd * 8 + 8,
                                0:64]
                            self.dv(v, "v", v.tensor_copy(dst, src))
                        vt_evac[rnd] = self.sig(v, "v2t")
                    self.EV(vte)
                self._vt_prev = vt_evac[1]

                for hh in range(2):
                    hof = hh * 64
                    for qb in range(NTB):
                        kcs = list(range(4 * (qb + 1)))
                        pairs = [(kcs[2 * i], kcs[2 * i + 1])
                                 for i in range(len(kcs) // 2)]
                        yslot = yi % 2
                        yof = yslot * 512
                        for pi, (k0, k1) in enumerate(pairs):
                            pp = pi % 2
                            ps = self.PS0 if pp == 0 else self.PS1
                            def mms(t, c=c, hof=hof, qb=qb, k0=k0, k1=k1,
                                    pp=pp, ps=ps, yi=yi, first=(hh == 0
                                    and qb == 0 and pi == 0)):
                                if first:
                                    self.wt(t, "s2t")
                                if pp in last_exp:
                                    self.wt(t, "s2t", last_exp[pp])
                                for half, kc in ((0, k0), (1, k1)):
                                    j = kc - 4 * qb
                                    band = 0 <= j
                                    self.dv(t, "t", t.matmul(
                                        ps[:, half * 512:half * 512 + 512],
                                        KB[hof:hof + 64,
                                           kc * 128:(kc + 1) * 128],
                                        QB[hof:hof + 64,
                                           qb * TB:(qb + 1) * TB],
                                        start=True, stop=not band))
                                    if band:
                                        self.dv(t, "t", t.matmul(
                                            ps[:, half * 512:half * 512 + 512],
                                            C["identb"][:, :],
                                            C["maskc"][:, j * TB:(j + 1) * TB],
                                            start=False, stop=True))
                                self.sig(t, "t2s")
                            self.ET(mms)
                            mt = self.tok("t2s")
                            def exps(s, k0=k0, pp=pp, ps=ps, pi=pi, mt=mt):
                                self.wt(s, "t2s", mt)
                                if pi in att_last_av:
                                    self.wt(s, "t2s", att_last_av[pi])
                                self.dv(s, "s", s.activation(
                                    ATT[:, k0 * TB:(k0 + 2) * TB], ps[:, :],
                                    AF.Exp, scale=0.125))
                                last_exp[pp] = self.sig(s, "s2t")
                            self.ES(exps)
                            et = self.tok("s2t")
                            def avs(t, c=c, hh=hh, k0=k0, k1=k1, et=et,
                                    yof=yof, yslot=yslot, qb=qb, pi=pi,
                                    npair=len(pairs)):
                                self.wt(t, "s2t", et)
                                if pi == 0 and hh == 0 and qb == 0:
                                    self.wt(t, "v2t", self._vt_prev)
                                if pi == 0 and yslot in last_evac_y:
                                    self.wt(t, "v2t", last_evac_y[yslot])
                                for kc in (k0, k1):
                                    vof = hh * (16 * 65) + kc * 65
                                    self.dv(t, "t", t.matmul(
                                        self.PS4[0:65, yof:yof + 512],
                                        VT[:, vof:vof + 65],
                                        ATT[:, kc * TB:(kc + 1) * TB],
                                        start=(kc == k0 and pi == 0),
                                        stop=(kc == k1 and pi == npair - 1)))
                                att_last_av[pi] = self.sig(t, "t2s")
                            self.ET(avs)
                        av_done = self.tok("t2s")

                        def srow(s, yof=yof, av_done=av_done):
                            self.wt(s, "t2s", av_done)
                            self.wt(s, "v2s")
                            self.dv(s, "s", s.copy(
                                self.SREC[:, :],
                                self.PS4[64:65, yof:yof + 512]))
                            self.sig(s, "s2v")
                        self.ES(srow)
                        st_ = self.tok("s2v")
                        def recip(v, st_=st_,
                                  lb=getattr(self, "_last_bcast_tok", None)):
                            self.wt(v, "s2v", st_)
                            if lb is not None:
                                self.wt(v, "t2v", lb)
                            with self.nc.allow_low_precision(
                                    "softmax 1/sum as bf16 bcast operand"):
                                self.dv(v, "v", v.reciprocal(
                                    self.RB16[:, :], self.SREC[:, :]))
                            self.sig(v, "v2s")
                            self.sig(v, "v2t")
                        self.EV(recip)
                        rt = self.tok("v2t")
                        def bcast(t, rt=rt):
                            self.wt(t, "v2t", rt)
                            if last_bcast_use is not None:
                                self.wt(t, "v2t", last_bcast_use)
                            self.dv(t, "t", t.matmul(
                                self.PS2[0:64, :],
                                C["ones_row_b"][0:1, 0:64], self.RB16[:, :],
                                start=True, stop=True))
                            self._last_bcast_tok = self.sig(t, "t2v")
                        self.ET(bcast)
                        bt = self.tok("t2v")
                        def bcc(s, bt=bt,
                                lye=last_evac_y.get((yi + 1) % 2)):
                            self.wt(s, "t2s", bt) if False else None
                            self.wt(s, "t2v_dummy", 0) if False else None
                            s.wait_ge(self.sem("t2v"), bt)
                            if lye is not None:
                                s.wait_ge(self.sem("v2t"), lye)
                            self.dv(s, "s", s.copy(self.BCSB[0:64, :],
                                                   self.PS2[0:64, :]))
                            self.sig(s, "s2t")
                        self.ES(bcc)
                        bct = self.tok("s2t")
                        def yevac(v, c=c, hof=hof, qb=qb, yof=yof,
                                  yslot=yslot, bct=bct):
                            self.wt(v, "s2t", bct)
                            self.dv(v, "v", v.tensor_mul(
                                YB[hof:hof + 64,
                                   c * T + qb * TB: c * T + (qb + 1) * TB],
                                self.PS4[0:64, yof:yof + 512],
                                self.BCSB[0:64, :]))
                            last_evac_y[yslot] = self.sig(v, "v2t")
                        self.EV(yevac)
                        last_bcast_use = last_evac_y[yslot]
                        yi += 1

            def qdone(t):
                self.sig(t, "t2y")
            self.ET(qdone)
            qd = self.tok("t2y")
            for d in range(NCH):
                half = (d % 2) * 1024
                def ld_wo(y, d=d, half=half, qd=qd):
                    if d == 0:
                        self.wt(y, "t2y", qd)
                        self.wt(y, "v2y", wq_cast_tok)
                    if d >= 2:
                        self.wt(y, "v2y", self._wotok[d - 2])
                    self.inc(y.dma_start(
                        out=self.TMP32[:, half:half + 1024],
                        in_=self.w["L0_attn_w_out"][d * 128:(d + 1) * 128, :]),
                        "dW" + str(d % 2), 16)
                self.EY(ld_wo)
                dt_ = self.tok("dW" + str(d % 2))
                def cast_wo(v, d=d, half=half, dt_=dt_):
                    self.wt(v, "dW" + str(d % 2), dt_)
                    self.dv(v, "v", v.tensor_copy(
                        WBIG[:, d * 1024:(d + 1) * 1024],
                        self.TMP32[:, half:half + 1024]))
                    self._wotok = getattr(self, "_wotok", {})
                    self._wotok[d] = self.sig(v, "v2y")
                self.EV(cast_wo)
            wo_tok = self.tok("v2y")
            yb_done = self.tok("v2t")

            oev = {}
            for co in range(NCH):
                for tb in range(NTB):
                    g = gslot; gslot += 1
                    def mmo(t, co=co, tb=tb, g=g):
                        if co == 0 and tb == 0:
                            self.wt(t, "v2y", wo_tok)
                            self.wt(t, "v2t", yb_done)
                        if g - 4 in oev:
                            self.wt(t, "v2t", oev[g - 4])
                        for d in range(NCH):
                            self.dv(t, "t", t.matmul(
                                self.slot(g),
                                WBIG[:, d * 1024 + co * 128:
                                     d * 1024 + co * 128 + 128],
                                YB[:, d * T + tb * TB: d * T + (tb + 1) * TB],
                                start=(d == 0), stop=(d == NCH - 1)))
                        self.sig(t, "t2v")
                    self.ET(mmo)
                    mt = self.tok("t2v")
                    def evo(v, co=co, tb=tb, g=g, mt=mt):
                        self.wt(v, "t2v", mt)
                        xs = self.X32[:, co * T + tb * TB:
                                      co * T + (tb + 1) * TB]
                        self.dv(v, "v", v.scalar_tensor_tensor(
                            xs, self.slot(g), 0.0, xs, ALU.bypass, ALU.add))
                        oev[g] = self.sig(v, "v2t")
                    self.EV(evo)

            self.layernorm("ln1", mirror=True)
            self.fence("tvsgy")


    def repl_col(self, src11, dst_col, tag, exact=False):
        C = self.CONST
        def r1(v, src11=src11, exact=exact):
            self.dv(v, "v", v.tensor_scalar(
                self.ROWX[:, :], C["ones_row_f"][:, :], src11, None,
                ALU.mult))
            with self.nc.allow_low_precision("repl via bf16 transpose"):
                self.dv(v, "v", v.tensor_copy(self.HLR[0:1, 0:128],
                                              self.ROWX[:, :]))
                if exact:
                    self.dv(v, "v", v.tensor_sub(
                        self.ROWY[:, :], self.ROWX[:, :],
                        self.HLR[0:1, 0:128]))
                    self.dv(v, "v", v.tensor_copy(
                        self.HLR[0:1, 128:256], self.ROWY[:, :]))
                    self.dv(v, "v", v.tensor_sub(
                        self.ROWX[:, :], self.ROWY[:, :],
                        self.HLR[0:1, 128:256]))
                    self.dv(v, "v", v.tensor_copy(
                        self.HLR[0:1, 256:384], self.ROWX[:, :]))
            self.sig(v, "v2t")
        self.EV(r1)
        vt = self.tok("v2t")
        def r2(t, vt=vt, exact=exact):
            self.wt(t, "v2t", vt)
            self.dv(t, "t", t.transpose(
                self.PSB[:, 0:1], self.HLR[0:1, 0:128],
                C["identb"][0:1, 0:1]))
            if exact:
                self.dv(t, "t", t.transpose(
                    self.PSB[:, 2:3], self.HLR[0:1, 128:256],
                    C["identb"][0:1, 0:1]))
                self.dv(t, "t", t.transpose(
                    self.PSB[:, 4:5], self.HLR[0:1, 256:384],
                    C["identb"][0:1, 0:1]))
            self.sig(t, "t2v")
        self.ET(r2)
        tt = self.tok("t2v")
        def r3(v, dst_col=dst_col, tt=tt, exact=exact):
            self.wt(v, "t2v", tt)
            self.dv(v, "v", v.tensor_copy(dst_col, self.PSB[:, 0:1]))
            if exact:
                self.dv(v, "v", v.tensor_add(dst_col, dst_col,
                                             self.PSB[:, 2:3]))
                self.dv(v, "v", v.tensor_add(dst_col, dst_col,
                                             self.PSB[:, 4:5]))
            self.sig(v, "v2t")
        self.EV(r3)
        return self.tok("v2t")

    def partition_amax(self, src_ap_128x1):
        def c1(v):
            with self.nc.allow_low_precision("amax via bf16 transpose"):
                self.dv(v, "v", v.tensor_copy(self.HLC[:, 0:1],
                                              src_ap_128x1))
            self.sig(v, "v2t")
        self.EV(c1)
        ct = self.tok("v2t")
        def t1(t, ct=ct):
            self.wt(t, "v2t", ct)
            self.dv(t, "t", t.transpose(
                self.PSB[0:1, 0:128], self.HLC[:, 0:1],
                self.CONST["identb"][:, :]))
            self.sig(t, "t2v")
        self.ET(t1)
        tt = self.tok("t2v")
        def v1(v, tt=tt):
            self.wt(v, "t2v", tt)
            self.dv(v, "v", v.reduce_max(
                self.AMX[0:1, 0:1], self.PSB[0:1, 0:128], AXX))
            self.sig(v, "v2g")
            self.sig(v, "v2t")
        self.EV(v1)

    def collective_max(self):
        vg = self.tok("v2g")
        def g1(g, vg=vg):
            self.wt(g, "v2g", vg)
            self.inc(g.dma_start(out=self.cc_in[:, :], in_=self.CCS[:, :]),
                     "dG", 16)
            self.wt(g, "dG")
            ins = g.collective_compute(
                "AllReduce", ALU.max,
                replica_groups=[list(range(NCORES))],
                ins=[self.cc_in[:, :].opt()], outs=[self.cc_out[:, :].opt()])
            self.inc(ins, "ccs")
            self.wt(g, "ccs")
            self.inc(g.dma_start(out=self.AMX[0:1, 1:2],
                                 in_=self.cc_out[0:1, 0:1]), "dG", 16)
            self.wt(g, "dG")
            self.sig(g, "g2v")
        self.EG(g1)
        return self.tok("g2v")

    def a16_scales(self, gv):
        def sc(v, gv=gv):
            self.wt(v, "g2v", gv)
            self.dv(v, "v", v.tensor_scalar(
                self.AMX[0:1, 1:2], self.AMX[0:1, 1:2], 1e-8, None,
                ALU.max))
            self.dv(v, "v", v.reciprocal(self.AMX[0:1, 2:3],
                                         self.AMX[0:1, 1:2]))
            self.dv(v, "v", v.tensor_scalar(
                self.AMX[0:1, 3:4], self.AMX[0:1, 2:3], 32767.0, None,
                ALU.mult))
            self.dv(v, "v", v.tensor_scalar(
                self.AMX[0:1, 4:5], self.AMX[0:1, 1:2], 1.0 / 32767.0,
                None, ALU.mult))
        self.EV(sc)
        self.repl_col(self.AMX[0:1, 3:4], self.SC[:, 0:1], "s")
        return self.repl_col(self.AMX[0:1, 4:5], self.SC[:, 1:2], "i")

    def p2_l0_ffn(self):
        C = self.CONST
        st = ExitStack()
        with st:
            WSTG = self.sb("WSTG", [128, 2048], F32, st)
            WQF = self.sb("WQF", [128, 1024], F32, st)
            WQT = self.sb("WQT", [128, 2048], BF16, st)
            WQ1 = self.sb("WQ1", [128, 2048], BF16, st)
            HTMP = self.sb("HTMP", [128, 2 * T], BF16, st)
            HQ16 = self.sb("HQ16", [128, T], BF16, st)
            HM = self.sb("HM", [128, NFF], F32, st)
            AMW = self.sb("AMW", [128, 4], F32, st)
            AMC = self.AMC = self.sb("AMC", [128, 1], F32, st)
            HQTB = self.sb("HQTB", [128, NFF * TB], BF16, st)
            WOSB = self.sb("WOSB", [128, 2 * NFF * 128], BF16, st)
            BOROWF = self.sb("BOROWF", [1, D], F32, st)
            BOROW = self.sb("BOROW", [1, 2 * D], BF16, st)

            def ldb(y):
                with self.nc.allow_non_contiguous_dma("tiny bias vectors"):
                    self.inc(y.dma_start(
                        out=self.BIN[:, :],
                        in_=self.w["L0_ffn_b_in"][:].rearrange(
                            "(f p) -> p f", p=128)), "dB", 16)
                self.inc(y.dma_start(
                    out=BOROWF[0:1, :],
                    in_=self.w["L0_ffn_b_out"][:].rearrange(
                        "(o f) -> o f", o=1)), "dB", 16)
            self.EY(ldb)
            def bprep(v):
                self.wt(v, "dB")
                with self.nc.allow_low_precision("bias hi/lo rows"):
                    self.dv(v, "v", v.tensor_copy(BOROW[0:1, 0:D],
                                                  BOROWF[0:1, :]))
                    self.dv(v, "v", v.tensor_sub(self.TMP32[0:1, 0:D],
                                                 BOROWF[0:1, :],
                                                 BOROW[0:1, 0:D]))
                    self.dv(v, "v", v.tensor_copy(BOROW[0:1, D:2 * D],
                                                  self.TMP32[0:1, 0:D]))
                self.sig(v, "v2y")
            self.EV(bprep)
            bprep_tok = self.tok("v2y")

            def am(v):
                self.dv(v, "v", v.memset(self.CCS[:, :], 0.0))
                self.dv(v, "v", v.memset(self.AMX[:, :], 0.0))
                for c in range(NCH):
                    self.dv(v, "v", v.reduce_max(
                        self.AM8[:, c:c + 1],
                        self.X32[:, c * T:(c + 1) * T], AXX,
                        apply_absolute_value=True))
                self.dv(v, "v", v.reduce_max(self.AMC[:, :],
                                             self.AM8[:, :], AXX))
                self.sig(v, "v2t")
            self.EV(am)
            self.partition_amax(self.AMC[:, 0:1])
            def amw(v):
                self.dv(v, "v", v.tensor_copy(self.CCS[0:1, 0:1],
                                              self.AMX[0:1, 0:1]))
                self.sig(v, "v2g")
            self.EV(amw)
            gv = self.collective_max()
            sc_tok = self.a16_scales(gv)
            self.dbg_put(self.AMX[0:1, 0:8], 0, 8)
            self.dbg_put(self.SC[0:1, 0:2], 8, 2)

            def xq(v, sc_tok=sc_tok):
                self.wt(v, "v2t", sc_tok)
                for c in range(NCH):
                    xs = self.X32[:, c * T:(c + 1) * T]
                    self.dv(v, "v", v.tensor_scalar(
                        self.TMP32[:, :], xs, self.SC[:, 0:1], MAGIC,
                        ALU.mult, ALU.add))
                    self.dv(v, "v", v.tensor_scalar(
                        self.XB[:, c * T:(c + 1) * T], self.TMP32[:, :],
                        MAGIC, self.SC[:, 1:2], ALU.subtract, ALU.mult))
                self.sig(v, "v2t")
                self.sig(v, "v2y")
            self.EV(xq)
            xq_tok = self.tok("v2t")

            wq2_ev = {}
            for fc in range(NFF):
                par = fc % 2
                def ldo(y, fc=fc, par=par):
                    if fc >= 2:
                        self.wt(y, "v2y", wq2_ev[fc - 2])
                    self.inc(y.dma_start(
                        out=WSTG[:, par * 1024:(par + 1) * 1024],
                        in_=self.w["L0_ffn_W_out"][
                            fc * 128:(fc + 1) * 128, :]),
                        "dW" + str(par), 16)
                self.EY(ldo)
                dt_ = self.tok("dW" + str(par))
                def qo(v, fc=fc, par=par, dt_=dt_):
                    self.wt(v, "dW" + str(par), dt_)
                    if fc >= 2:
                        self.wt(v, "dG")
                    src = WSTG[:, par * 1024:(par + 1) * 1024]
                    self.dv(v, "v", v.reduce_max(
                        AMW[:, 0:1], src, AXX, apply_absolute_value=True))
                    self.dv(v, "v", v.tensor_scalar(
                        AMW[:, 0:1], AMW[:, 0:1], 1e-8, None, ALU.max))
                    self.dv(v, "v", v.reciprocal(AMW[:, 1:2], AMW[:, 0:1]))
                    self.dv(v, "v", v.tensor_scalar(
                        AMW[:, 2:3], AMW[:, 1:2], 127.0, None, ALU.mult))
                    self.dv(v, "v", v.tensor_scalar(
                        AMW[:, 3:4], AMW[:, 0:1], 1.0 / 127.0, None,
                        ALU.mult))
                    self.dv(v, "v", v.tensor_scalar(
                        WQF[:, :], src, AMW[:, 2:3], MAGIC, ALU.mult,
                        ALU.add))
                    self.dv(v, "v", v.tensor_scalar(
                        WQT[:, par * 1024:(par + 1) * 1024], WQF[:, :],
                        MAGIC, AMW[:, 3:4], ALU.subtract, ALU.mult))
                    wq2_ev[fc] = self.sig(v, "v2y")
                    self.sig(v, "v2g")
                self.EV(qo)
                vg2 = self.tok("v2g")
                def sto(g, fc=fc, par=par, vg2=vg2):
                    self.wt(g, "v2g", vg2)
                    self.inc(g.dma_start(
                        out=self.wq2d[fc, :, :],
                        in_=WQT[:, par * 1024:(par + 1) * 1024]), "dG", 16)
                self.EY(sto)

            evtok = {}
            qwtok = {}
            g = 0
            for f in range(NFF):
                par = f % 2
                def ldi(y, f=f, par=par):
                    if f < 2:
                        self.wt(y, "v2y")
                    else:
                        self.wt(y, "v2y", qwtok[f - 2])
                    self.inc(y.dma_start(
                        out=WSTG[:, par * 1024:(par + 1) * 1024],
                        in_=self.w["L0_ffn_W_in"][
                            f * 128:(f + 1) * 128, :]),
                        "dW" + str(par), 16)
                self.EY(ldi)
                dt_ = self.tok("dW" + str(par))
                def qw(v, f=f, par=par, dt_=dt_):
                    self.wt(v, "dW" + str(par), dt_)
                    srcw = WSTG[:, par * 1024:(par + 1) * 1024]
                    self.dv(v, "v", v.reduce_max(
                        AMW[:, 0:1], srcw, AXX,
                        apply_absolute_value=True))
                    self.dv(v, "v", v.tensor_scalar(
                        AMW[:, 0:1], AMW[:, 0:1], 1e-8, None, ALU.max))
                    self.dv(v, "v", v.reciprocal(AMW[:, 1:2], AMW[:, 0:1]))
                    self.dv(v, "v", v.tensor_scalar(
                        AMW[:, 2:3], AMW[:, 1:2], 127.0, None, ALU.mult))
                    self.dv(v, "v", v.tensor_scalar(
                        AMW[:, 3:4], AMW[:, 0:1], 1.0 / 127.0, None,
                        ALU.mult))
                    self.dv(v, "v", v.tensor_scalar(
                        WQF[:, :], srcw, AMW[:, 2:3], MAGIC,
                        ALU.mult, ALU.add))
                    self.dv(v, "v", v.tensor_scalar(
                        WQT[:, 0:1024], WQF[:, :],
                        MAGIC, AMW[:, 3:4], ALU.subtract, ALU.mult))
                    qwtok[f] = self.sig(v, "v2y")
                    self.sig(v, "v2t")
                self.EV(qw)
                q1 = self.tok("v2t")
                def tr2(t, f=f, q1=q1):
                    self.wt(t, "v2t", q1)
                    for j in range(8):
                        self.dv(t, "t", t.transpose(
                            self.PSB[:, j * 128:(j + 1) * 128],
                            WQT[:, j * 128:(j + 1) * 128],
                            C["identb"][:, :]))
                    self.sig(t, "t2v")
                self.ET(tr2)
                t2 = self.tok("t2v")
                def wv(v, f=f, par=par, t2=t2):
                    self.wt(v, "t2v", t2)
                    self.dv(v, "v", v.tensor_copy(
                        WQ1[:, par * 1024:(par + 1) * 1024],
                        self.PSB[:, 0:1024]))
                    self.sig(v, "v2t")
                self.EV(wv)
                wt_ = self.tok("v2t")

                for tb in range(NTB):
                    gg = g; g += 1
                    def mmh(t, f=f, par=par, tb=tb, gg=gg, wt_=wt_,
                            xq_tok=xq_tok):
                        if f == 0 and tb == 0:
                            self.wt(t, "v2t", xq_tok)
                        self.wt(t, "v2t", wt_)
                        if gg >= 4:
                            self.wt(t, "s2t", evtok[gg - 4])
                        for d in range(NCH):
                            self.dv(t, "t", t.matmul(
                                self.slot2(gg),
                                WQ1[:, par * 1024 + d * 128:
                                    par * 1024 + (d + 1) * 128],
                                self.XB[:, d * T + tb * TB:
                                        d * T + (tb + 1) * TB],
                                start=(d == 0), stop=(d == NCH - 1)))
                        self.sig(t, "t2s")
                    self.ET(mmh)
                    mt = self.tok("t2s")
                    def evh(s, f=f, par=par, tb=tb, gg=gg, mt=mt):
                        self.wt(s, "t2s", mt)
                        if f == 0 and tb == 0:
                            self.wt(s, "dB")
                        if f >= 2 and tb == 0:
                            self.wt(s, "dG")
                            self.wt(s, "v2s")
                        self.dv(s, "s", s.activation(
                            HTMP[:, par * T + tb * TB:
                                 par * T + (tb + 1) * TB],
                            self.slot2(gg), AF.Gelu,
                            bias=self.BIN[:, f:f + 1]))
                        evtok[gg] = self.sig(s, "s2t")
                        self.sig(s, "s2v")
                        self.sig(s, "s2g")
                    self.ES(evh)
                sg = self.tok("s2g")
                def hmax(v, f=f, par=par):
                    self.wt(v, "s2v")
                    self.dv(v, "v", v.reduce_max(
                        HM[:, f:f + 1], HTMP[:, par * T:(par + 1) * T],
                        AXX, apply_absolute_value=True))
                    self.sig(v, "v2s")
                self.EV(hmax)
                def sth(gp, f=f, par=par, sg=sg):
                    self.wt(gp, "s2g", sg)
                    for tb in range(NTB):
                        self.inc(gp.dma_start(
                            out=self.hid[f * 128:(f + 1) * 128,
                                         tb * TB:(tb + 1) * TB],
                            in_=HTMP[:, par * T + tb * TB:
                                     par * T + (tb + 1) * TB]), "dG", 16)
                self.EY(sth)

            def am2(v):
                self.dv(v, "v", v.reduce_max(self.AMC[:, :], HM[:, :], AXX))
                self.sig(v, "v2t")
            self.EV(am2)
            self.partition_amax(self.AMC[:, 0:1])
            def amw2(v):
                self.dv(v, "v", v.tensor_copy(self.CCS[0:1, 0:1],
                                              self.AMX[0:1, 0:1]))
                self.sig(v, "v2g")
            self.EV(amw2)
            gv2 = self.collective_max()
            sc2_tok = self.a16_scales(gv2)
            self.dbg_put(self.AMX[0:1, 0:8], 16, 8)
            self.dbg_put(self.SC[0:1, 0:2], 24, 2)

            hq_ev = {}
            for f in range(NFF):
                par = f % 2
                def ldh(g_, f=f, par=par):
                    if f == 0:
                        self.wt(g_, "dG")
                    if f >= 2:
                        self.wt(g_, "v2g", hq_ev[f - 2])
                    self.inc(g_.dma_start(
                        out=HTMP[:, par * T:(par + 1) * T],
                        in_=self.hid[f * 128:(f + 1) * 128, :]),
                        "dGh" + str(par), 16)
                self.EY(ldh)
                ht = self.tok("dGh" + str(par))
                def qh(v, f=f, par=par, ht=ht, sc2_tok=sc2_tok):
                    self.wt(v, "dGh" + str(par), ht)
                    if f == 0:
                        self.wt(v, "v2t", sc2_tok)
                    if f >= 1:
                        self.wt(v, "dGq")
                    self.dv(v, "v", v.tensor_scalar(
                        self.TMP32[:, :], HTMP[:, par * T:(par + 1) * T],
                        self.SC[:, 0:1], MAGIC, ALU.mult, ALU.add))
                    self.dv(v, "v", v.tensor_scalar(
                        HQ16[:, :], self.TMP32[:, :], MAGIC,
                        self.SC[:, 1:2], ALU.subtract, ALU.mult))
                    hq_ev[f] = self.sig(v, "v2g")
                self.EV(qh)
                vg_ = self.tok("v2g")
                def sthq(g_, f=f, vg_=vg_):
                    self.wt(g_, "v2g", vg_)
                    self.inc(g_.dma_start(
                        out=self.hq[f * 128:(f + 1) * 128, :],
                        in_=HQ16[:, :]), "dGq", 16)
                self.EY(sthq)

            oev = {}
            g2 = 0
            for tb in range(NTB):
                tpar = tb % 2
                def ldq(y, tb=tb, tpar=tpar):
                    if tb == 0:
                        self.wt(y, "dGq")
                    if tb >= 1:
                        self.wt(y, "t2v")
                    for f in range(NFF):
                        self.inc(y.dma_start(
                            out=HQTB[:, f * TB:(f + 1) * TB],
                            in_=self.hq[f * 128:(f + 1) * 128,
                                        tb * TB:(tb + 1) * TB]),
                            "dHq" + str(tpar), 16)
                self.EY(ldq)
                qt = self.tok("dHq" + str(tpar))
                for co in range(NCH):
                    cpar = (tb * NCH + co) % 2
                    def ldwo(y, tb=tb, co=co, cpar=cpar):
                        if tb == 0 and co == 0:
                            self.wt(y, "dG")
                        if tb * NCH + co >= 2:
                            self.wt(y, "t2y", self._wo2tok[(tb, co)])
                        self.inc(y.dma_start(
                            out=WOSB[:, cpar * 4096:(cpar + 1) * 4096]
                            .rearrange("p (f j) -> p f j", f=NFF),
                            in_=self.wq2d[:, :, co * 128:(co + 1) * 128]
                            .rearrange("f p j -> p f j")),
                            "dWo" + str(cpar), 16)
                    self.EY(ldwo)
                    wt2 = self.tok("dWo" + str(cpar))
                    def mmo(t, tb=tb, co=co, cpar=cpar, g2=g2, qt=qt,
                            wt2=wt2):
                        if g2 == 0:
                            self.wt(t, "s2t")
                            self.wt(t, "v2y", bprep_tok)
                        self.wt(t, "dHq" + str(tb % 2), qt)
                        self.wt(t, "dWo" + str(cpar), wt2)
                        if g2 >= 4:
                            self.wt(t, "v2t", oev[g2 - 4])
                        for f in range(NFF):
                            self.dv(t, "t", t.matmul(
                                self.slot(g2),
                                WOSB[:, cpar * 4096 + f * 128:
                                     cpar * 4096 + (f + 1) * 128],
                                HQTB[:, f * TB:(f + 1) * TB],
                                start=(f == 0), stop=False))
                        for hl in range(2):
                            self.dv(t, "t", t.matmul(
                                self.slot(g2),
                                BOROW[0:1, hl * D + co * 128:
                                      hl * D + (co + 1) * 128],
                                C["ones_tb_b"][0:1, :],
                                start=False, stop=(hl == 1)))
                        self._wo2tok = getattr(self, "_wo2tok", {})
                        key = (tb + (0 if co + 2 < NCH else 1),
                               (co + 2) % NCH)
                        self._wo2tok[key] = self.sig(t, "t2y")
                        self.sig(t, "t2v")
                    self.ET(mmo)
                    mt = self.tok("t2v")
                    def evo(v, tb=tb, co=co, g2=g2, mt=mt):
                        self.wt(v, "t2v", mt)
                        xs = self.X32[:, co * T + tb * TB:
                                      co * T + (tb + 1) * TB]
                        self.dv(v, "v", v.scalar_tensor_tensor(
                            xs, self.slot(g2), 0.0, xs, ALU.bypass,
                            ALU.add))
                        oev[g2] = self.sig(v, "v2t")
                    self.EV(evo)
                    g2 += 1

            self.layernorm("ln2", mirror=True)
            self.fence("tvsgy")


    def p3_l1_attn(self):
        C = self.CONST
        st = ExitStack()
        with st:
            W3T = self.sb("W3T", [128, 2 * 1024], BF16, st)
            WG1T = self.sb("WG1T", [128, 1024], BF16, st)
            WG2T = self.sb("WG2T", [128, 64], BF16, st)
            HGT = self.sb("HGT", [128, T], BF16, st)
            QB = self.sb("QBp3", [128, T], BF16, st)
            KB = self.sb("KBp3", [128, T], BF16, st)
            VTMP = self.sb("VTMPp3", [128, T], BF16, st)
            VT = self.sb("VTp3", [128, 2 * 16 * 65], BF16, st)
            ATT = self.sb("ATTp3", [128, 16 * TB], BF16, st)
            YB = self.sb("YBp3", [128, NCH * T], BF16, st)
            SQF = HGT
            RQK = self.TMP32[0:2, :]
            RQKB = self.sb("RQKB", [2, T], BF16, st)
            BROWS = self.sb("BROWS", [1, 4608], BF16, st)
            GR = self.sb("GR", [NH, 4], F32, st)

            rounds = [
                [("L1_attn_bq", 0, 1024), ("L1_attn_bk", 1024, 1024)],
                [("L1_attn_bv", 2048, 1024), ("L1_attn_bo", 3072, 1024)],
                [("L1_attn_bg1", 4096, 512)],
            ]
            for ri, rnd in enumerate(rounds):
                def ldb3(y, ri=ri, rnd=rnd):
                    if ri >= 1:
                        self.wt(y, "v2t")
                    for nm, off, ln in rnd:
                        self.inc(y.dma_start(
                            out=self.TMP32[0:1, off - ri * 2048:
                                           off - ri * 2048 + ln],
                            in_=self.w[nm][:].rearrange(
                                "(o d) -> o d", o=1)), "dB", 16)
                    if ri == 2:
                        with self.nc.allow_non_contiguous_dma("tiny"):
                            self.inc(y.dma_start(
                                out=self.BG2[:, :],
                                in_=self.w["L1_attn_bg2"][:].rearrange(
                                    "(d o) -> d o", o=1)), "dB", 16)
                            self.inc(y.dma_start(
                                out=self.GIN[:, :],
                                in_=self.w["L1_attn_g"][:].rearrange(
                                    "(d o) -> d o", o=1)), "dB", 16)
                self.EY(ldb3)
                bt_ = self.tok("dB")
                def castb(v, ri=ri, bt_=bt_):
                    self.wt(v, "dB", bt_)
                    ln = 512 if ri == 2 else 2048
                    self.dv(v, "v", v.tensor_copy(
                        BROWS[0:1, ri * 2048: ri * 2048 + ln],
                        self.TMP32[0:1, 0:ln]))
                    self.sig(v, "v2y")
                    self.sig(v, "v2t")
                self.EV(castb)
            brows_tok = self.tok("v2t")

            def ldg2(y, brows_tok=brows_tok):
                self.wt(y, "v2y")
                self.inc(y.dma_start(
                    out=self.TMP32[:, 0:64].rearrange(
                        "p (b h) -> p b h", b=4),
                    in_=self.w["L1_attn_Wg2"][:, :].rearrange(
                        "(b p) h -> p b h", p=128)), "dW0", 16)
            self.EY(ldg2)
            g2t = self.tok("dW0")
            def cg2(v, g2t=g2t):
                self.wt(v, "dW0", g2t)
                self.dv(v, "v", v.tensor_copy(WG2T[:, :],
                                              self.TMP32[:, 0:64]))
                self.sig(v, "v2t")
            self.EV(cg2)
            wg2_tok = self.tok("v2t")

            glslots = [(self.PS4, 0), (self.PS4, 512),
                       (self.PS0, 0), (self.PS0, 512)]
            hg_ev = {}
            for fc in range(4):
                par = fc % 2
                def ldg1(y, fc=fc, par=par):
                    if fc < 2:
                        self.wt(y, "v2t")
                    else:
                        self.wt(y, "v2y", self._g1tok[fc - 2])
                    self.inc(y.dma_start(
                        out=self.TMP32[:, par * 1024:(par + 1) * 1024]
                        .rearrange("p (dd j) -> p dd j", dd=8),
                        in_=self.w["L1_attn_Wg1"][
                            :, fc * 128:(fc + 1) * 128].rearrange(
                            "(dd p) j -> p dd j", p=128)),
                        "dW" + str(par), 16)
                self.EY(ldg1)
                dt_ = self.tok("dW" + str(par))
                def cg1(v, fc=fc, par=par, dt_=dt_):
                    self.wt(v, "dW" + str(par), dt_)
                    self.wt(v, "t2v")
                    self.dv(v, "v", v.tensor_copy(
                        WG1T[:, :],
                        self.TMP32[:, par * 1024:(par + 1) * 1024]))
                    self._g1tok = getattr(self, "_g1tok", {})
                    self._g1tok[fc] = self.sig(v, "v2y")
                    self.sig(v, "v2t")
                self.EV(cg1)
                wg1_tok = self.tok("v2t")

                for tb in range(NTB):
                    sl = (self.PS1, (tb % 2) * 512)
                    def mmg(t, fc=fc, tb=tb, sl=sl, wg1_tok=wg1_tok,
                            brows_tok=brows_tok):
                        if fc == 0 and tb == 0:
                            self.wt(t, "v2t", brows_tok)
                        self.wt(t, "v2t", wg1_tok)
                        if tb >= 2 or fc > 0:
                            self.wt(t, "s2t")
                        ps, off = sl
                        for d in range(NCH):
                            self.dv(t, "t", t.matmul(
                                ps[:, off:off + 512],
                                WG1T[:, d * 128:(d + 1) * 128],
                                self.XB[:, d * T + tb * TB:
                                        d * T + (tb + 1) * TB],
                                start=(d == 0), stop=False))
                        self.dv(t, "t", t.matmul(
                            ps[:, off:off + 512],
                            BROWS[0:1, 4096 + fc * 128:4096 + (fc + 1) * 128],
                            C["ones_tb_b"][:, :], start=False, stop=True))
                        self.sig(t, "t2s")
                    self.ET(mmg)
                    mt = self.tok("t2s")
                    def evg(s, fc=fc, tb=tb, sl=sl, mt=mt):
                        self.wt(s, "t2s", mt)
                        ps, off = sl
                        self.dv(s, "s", s.activation(
                            HGT[:, tb * TB:(tb + 1) * TB],
                            ps[:, off:off + 512], AF.Gelu))
                        self.sig(s, "s2t")
                        self.sig(s, "s2v")
                    self.ES(evg)
                    glsl = glslots[tb]
                    def mmgl(t, fc=fc, tb=tb, glsl=glsl,
                             wg2_tok=wg2_tok):
                        if fc == 0 and tb == 0:
                            self.wt(t, "v2t", wg2_tok)
                        self.wt(t, "s2t")
                        ps, off = glsl
                        self.dv(t, "t", t.matmul(
                            ps[0:16, off:off + 512],
                            WG2T[:, fc * 16:(fc + 1) * 16],
                            HGT[:, tb * TB:(tb + 1) * TB],
                            start=(fc == 0), stop=(fc == 3)))
                        self.sig(t, "t2v")
                    self.ET(mmgl)
                hg_ev[fc] = self.tok("t2v")

            gl_tok = self.tok("t2v")
            def glred(v, gl_tok=gl_tok):
                self.wt(v, "t2v", gl_tok)
                for tb in range(NTB):
                    ps, off = glslots[tb]
                    self.dv(v, "v", v.reduce_sum(
                        GR[:, tb:tb + 1], ps[0:16, off:off + 512], AXX))
                self.dv(v, "v", v.reduce_sum(self.G2[:, 1:2], GR[:, :], AXX))
                self.dv(v, "v", v.tensor_scalar(
                    self.G2[:, 1:2], self.G2[:, 1:2], 1.0 / T, None,
                    ALU.mult))
                self.dv(v, "v", v.tensor_add(self.G2[:, 1:2],
                                             self.G2[:, 1:2],
                                             self.BG2[:, :]))
                self.dv(v, "v", v.tensor_copy(self.G2[:, 0:1],
                                              self.GIN[:, :]))
                self.sig(v, "v2s")
            self.EV(glred)
            gv_ = self.tok("v2s")
            def gsig(s, gv_=gv_):
                self.wt(s, "v2s", gv_)
                self.dv(s, "s", s.activation(self.G2[:, :], self.G2[:, :],
                                             AF.Sigmoid))
                self.sig(s, "s2v")
            self.ES(gsig)
            gs_ = self.tok("s2v")
            def gmul(v, gs_=gs_):
                self.wt(v, "s2v", gs_)
                self.dv(v, "v", v.tensor_mul(self.GT[:, :], self.G2[:, 0:1],
                                             self.G2[:, 1:2]))
                self.sig(v, "v2t")
            self.EV(gmul)
            gm_ = self.tok("v2t")
            def gtr(t, gm_=gm_):
                self.wt(t, "v2t", gm_)
                self.dv(t, "t", t.transpose(
                    self.PSB[0:1, 64:80], self.GT[:, :],
                    C["identb"][0:16, 0:16]))
                self.sig(t, "t2v")
            self.ET(gtr)
            gt_ = self.tok("t2v")
            def gev(v, gt_=gt_):
                self.wt(v, "t2v", gt_)
                self.dv(v, "v", v.tensor_copy(self.GROW[:, :],
                                              self.PSB[0:1, 64:80]))
                self.sig(v, "v2t")
            self.EV(gev)
            gr_ = self.tok("v2t")
            def gbc(t, gr_=gr_):
                self.wt(t, "v2t", gr_)
                self.dv(t, "t", t.matmul(
                    self.PS2[:, 0:16], C["ones_row_b"][0:1, :],
                    self.GROW[:, :], start=True, stop=True))
                self.sig(t, "t2v")
            self.ET(gbc)
            gb_ = self.tok("t2v")
            def gbe(v, gb_=gb_):
                self.wt(v, "t2v", gb_)
                self.dv(v, "v", v.tensor_copy(self.GBS[:, :],
                                              self.PS2[:, 0:16]))
                self.sig(v, "v2t")
            self.EV(gbe)
            gbs_tok = self.tok("v2t")

            def ones_cols3(v):
                ap = VT[:, :].rearrange("p (n c) -> p n c", c=65)[:, :, 64:65]
                self.dv(v, "v", v.memset(ap, 1.0))
                self.sig(v, "v2t")
            self.EV(ones_cols3)

            last_exp = {}
            last_evac_y = {}
            last_bcast_use = None
            self._last_bcast_tok3 = None
            yi = 0
            gslot = 0
            evtok3 = {}
            att_last_av = {}
            for c in range(NCH):
                for wi, (nm, dst, brow_off) in enumerate((
                        ("L1_attn_Wq", QB, 0), ("L1_attn_Wk", KB, 1024),
                        ("L1_attn_Wv", VTMP, 2048))):
                    par = (c * 3 + wi) % 2
                    wslot = wi % 2
                    def ldw3(y, nm=nm, par=par, c=c, wi=wi):
                        self.wt(y, "v2t")
                        self.inc(y.dma_start(
                            out=self.TMP32[:, par * 1024:(par + 1) * 1024]
                            .rearrange("p (dd j) -> p dd j", dd=8),
                            in_=self.w[nm][:, c * 128:(c + 1) * 128]
                            .rearrange("(dd p) j -> p dd j", p=128)),
                            "dW" + str(par), 16)
                    self.EY(ldw3)
                    dt_ = self.tok("dW" + str(par))
                    def cw3(v, par=par, wslot=wslot, dt_=dt_):
                        self.wt(v, "dW" + str(par), dt_)
                        self.wt(v, "s2t")
                        self.dv(v, "v", v.tensor_copy(
                            W3T[:, wslot * 1024:(wslot + 1) * 1024],
                            self.TMP32[:, par * 1024:(par + 1) * 1024]))
                        self.sig(v, "v2t")
                    self.EV(cw3)
                    w3_tok = self.tok("v2t")

                    for tb in range(NTB):
                        gg = gslot; gslot += 1
                        def mmp3(t, wi=wi, wslot=wslot, dst=dst,
                                 brow_off=brow_off,
                                 tb=tb, gg=gg, c=c, w3_tok=w3_tok):
                            if tb == 0:
                                self.wt(t, "v2t", w3_tok)
                            if gg >= 4:
                                self.wt(t, "s2t", evtok3.get(gg - 4, 0))
                            for d in range(NCH):
                                self.dv(t, "t", t.matmul(
                                    self.slot(gg),
                                    W3T[:, wslot * 1024 + d * 128:
                                        wslot * 1024 + (d + 1) * 128],
                                    self.XB[:, d * T + tb * TB:
                                            d * T + (tb + 1) * TB],
                                    start=(d == 0), stop=False))
                            self.dv(t, "t", t.matmul(
                                self.slot(gg),
                                BROWS[0:1, brow_off + c * 128:
                                      brow_off + (c + 1) * 128],
                                C["ones_tb_b"][:, :], start=False,
                                stop=True))
                            self.sig(t, "t2s")
                        self.ET(mmp3)
                        mt = self.tok("t2s")
                        def evp3(s, dst=dst, tb=tb, gg=gg, mt=mt):
                            self.wt(s, "t2s", mt)
                            self.dv(s, "s", s.copy(
                                dst[:, tb * TB:(tb + 1) * TB],
                                self.slot(gg)))
                            evtok3[gg] = self.sig(s, "s2t")
                        self.ES(evp3)

                for dst in (QB, KB):
                    qk_ev = self.tok("s2t")
                    def sq(v, dst=dst, qk_ev=qk_ev):
                        self.wt(v, "s2t", qk_ev)
                        self.wt(v, "t2v")
                        self.dv(v, "v", v.tensor_mul(SQF[:, :], dst[:, :],
                                                     dst[:, :]))
                        self.sig(v, "v2t")
                    self.EV(sq)
                    sqt = self.tok("v2t")
                    for tb in range(NTB):
                        def mmsq(t, tb=tb, sqt=sqt):
                            self.wt(t, "v2t", sqt)
                            if tb > 0:
                                self.wt(t, "v2t")
                            self.dv(t, "t", t.matmul(
                                self.PS4[0:2, 0:512], C["ind2"][:, :],
                                SQF[:, tb * TB:(tb + 1) * TB],
                                start=True, stop=True))
                            self.sig(t, "t2v")
                        self.ET(mmsq)
                        st_ = self.tok("t2v")
                        def cprow(v, tb=tb, st_=st_):
                            self.wt(v, "t2v", st_)
                            self.dv(v, "v", v.tensor_copy(
                                RQK[:, tb * TB:(tb + 1) * TB],
                                self.PS4[0:2, 0:512]))
                            self.sig(v, "v2t")
                        self.EV(cprow)
                    rt_ = self.tok("v2t")
                    def rows3(s, rt_=rt_):
                        self.wt(s, "v2t", rt_)
                        self.dv(s, "s", s.activation(
                            RQK[:, :], RQK[:, :], AF.Ln,
                            bias=C["epsc"][0:2, 1:2]))
                        self.dv(s, "s", s.activation(
                            RQK[:, :], RQK[:, :], AF.Exp, scale=-0.5))
                        self.sig(s, "s2v")
                    self.ES(rows3)
                    rs_ = self.tok("s2v")
                    def rmin(v, rs_=rs_):
                        self.wt(v, "s2v", rs_)
                        self.dv(v, "v", v.tensor_scalar(
                            RQK[:, :], RQK[:, :], 1e12, None, ALU.min))
                        with self.nc.allow_low_precision("rq/rk bf16 bcast"):
                            self.dv(v, "v", v.tensor_copy(RQKB[:, :],
                                                          RQK[:, :]))
                        self.sig(v, "v2t")
                    self.EV(rmin)
                    rb_ = self.tok("v2t")
                    for tb in range(NTB):
                        def bcq(t, tb=tb, rb_=rb_):
                            self.wt(t, "v2t", rb_)
                            if tb > 0:
                                self.wt(t, "v2t")
                            self.dv(t, "t", t.matmul(
                                self.PS2[:, :], C["indt"][:, :],
                                RQKB[:, tb * TB:(tb + 1) * TB],
                                start=True, stop=True))
                            self.sig(t, "t2v")
                        self.ET(bcq)
                        bt_ = self.tok("t2v")
                        def apq(v, dst=dst, tb=tb, bt_=bt_):
                            self.wt(v, "t2v", bt_)
                            self.dv(v, "v", v.tensor_mul(
                                dst[:, tb * TB:(tb + 1) * TB],
                                dst[:, tb * TB:(tb + 1) * TB],
                                self.PS2[:, :]))
                            self.sig(v, "v2t")
                        self.EV(apq)

                vt_evac = {}
                for rnd in range(2):
                    def vtr3(t, c=c, rnd=rnd):
                        self.wt(t, "s2t")
                        self.wt(t, "v2t")
                        for j in range(8):
                            kc = rnd * 8 + j
                            self.dv(t, "t", t.transpose(
                                self.PSB[:, j * 128:(j + 1) * 128],
                                VTMP[:, kc * 128:(kc + 1) * 128],
                                C["identb"][:, :]))
                        self.sig(t, "t2v")
                    self.ET(vtr3)
                    tt = self.tok("t2v")
                    def vte3(v, c=c, rnd=rnd, tt=tt, gbs_tok=gbs_tok):
                        self.wt(v, "t2v", tt)
                        if c == 0 and rnd == 0:
                            self.wt(v, "v2t", gbs_tok)
                        for hh in range(2):
                            src = self.PSB[:, 0:1024].rearrange(
                                "p (j k) -> p j k", j=8)[:, :,
                                hh * 64:hh * 64 + 64]
                            dstv = VT[:, :].rearrange(
                                "p (n k) -> p n k", k=65)[
                                :, hh * 16 + rnd * 8: hh * 16 + rnd * 8 + 8,
                                0:64]
                            self.dv(v, "v", v.tensor_scalar(
                                dstv, src, self.GBS[:, 2 * c + hh:
                                                    2 * c + hh + 1],
                                None, ALU.mult))
                        vt_evac[rnd] = self.sig(v, "v2t")
                    self.EV(vte3)
                self._vt_prev3 = vt_evac[1]

                for hh in range(2):
                    hof = hh * 64
                    for qb in range(NTB):
                        kcs = list(range(16))
                        pairs = [(2 * i, 2 * i + 1) for i in range(8)]
                        yslot = yi % 2
                        yof = yslot * 512
                        for pi, (k0, k1) in enumerate(pairs):
                            pp = pi % 2
                            ps = self.PS0 if pp == 0 else self.PS1
                            def mms3(t, hof=hof, qb=qb, k0=k0, k1=k1,
                                     pp=pp, ps=ps,
                                     first=(hh == 0 and qb == 0 and
                                            pi == 0)):
                                if first:
                                    self.wt(t, "s2t")
                                    self.wt(t, "v2t")
                                if pp in last_exp:
                                    self.wt(t, "s2t", last_exp[pp])
                                for half, kc in ((0, k0), (1, k1)):
                                    self.dv(t, "t", t.matmul(
                                        ps[:, half * 512:half * 512 + 512],
                                        KB[hof:hof + 64,
                                           kc * 128:(kc + 1) * 128],
                                        QB[hof:hof + 64,
                                           qb * TB:(qb + 1) * TB],
                                        start=True, stop=True))
                                self.sig(t, "t2s")
                            self.ET(mms3)
                            mt = self.tok("t2s")
                            def exps3(s, k0=k0, pp=pp, ps=ps, pi=pi, mt=mt):
                                self.wt(s, "t2s", mt)
                                if pi in att_last_av:
                                    self.wt(s, "t2s", att_last_av[pi])
                                self.dv(s, "s", s.activation(
                                    ATT[:, k0 * TB:(k0 + 2) * TB],
                                    ps[:, :], AF.Exp, scale=0.125))
                                last_exp[pp] = self.sig(s, "s2t")
                            self.ES(exps3)
                            et = self.tok("s2t")
                            def avs3(t, hh=hh, k0=k0, k1=k1, et=et,
                                     yof=yof, yslot=yslot, pi=pi):
                                self.wt(t, "s2t", et)
                                if pi == 0 and hh == 0 and qb == 0:
                                    self.wt(t, "v2t", self._vt_prev3)
                                if pi == 0 and yslot in last_evac_y:
                                    self.wt(t, "v2t", last_evac_y[yslot])
                                for kc in (k0, k1):
                                    vof = hh * (16 * 65) + kc * 65
                                    self.dv(t, "t", t.matmul(
                                        self.PS4[0:65, yof:yof + 512],
                                        VT[:, vof:vof + 65],
                                        ATT[:, kc * TB:(kc + 1) * TB],
                                        start=(kc == k0 and pi == 0),
                                        stop=(kc == k1 and pi == 7)))
                                att_last_av[pi] = self.sig(t, "t2s")
                            self.ET(avs3)
                        av_done = self.tok("t2s")

                        def srow3(s, yof=yof, av_done=av_done):
                            self.wt(s, "t2s", av_done)
                            self.wt(s, "v2s")
                            self.dv(s, "s", s.copy(
                                self.SREC[:, :],
                                self.PS4[64:65, yof:yof + 512]))
                            self.sig(s, "s2v")
                        self.ES(srow3)
                        st_ = self.tok("s2v")
                        def recip3(v, st_=st_, lb=self._last_bcast_tok3):
                            self.wt(v, "s2v", st_)
                            if lb is not None:
                                self.wt(v, "t2v", lb)
                            with self.nc.allow_low_precision(
                                    "softmax 1/sum bf16 bcast"):
                                self.dv(v, "v", v.reciprocal(
                                    self.RB16[:, :], self.SREC[:, :]))
                            self.sig(v, "v2s")
                            self.sig(v, "v2t")
                        self.EV(recip3)
                        rt = self.tok("v2t")
                        def bcast3(t, rt=rt):
                            self.wt(t, "v2t", rt)
                            if last_bcast_use is not None:
                                self.wt(t, "v2t", last_bcast_use)
                            self.dv(t, "t", t.matmul(
                                self.PS2[0:64, :],
                                C["ones_row_b"][0:1, 0:64],
                                self.RB16[:, :], start=True, stop=True))
                            self._last_bcast_tok3 = self.sig(t, "t2v")
                        self.ET(bcast3)
                        bt = self.tok("t2v")
                        def bcc3(s, bt=bt,
                                 lye=last_evac_y.get((yi + 1) % 2)):
                            s.wait_ge(self.sem("t2v"), bt)
                            if lye is not None:
                                s.wait_ge(self.sem("v2t"), lye)
                            self.dv(s, "s", s.copy(self.BCSB[0:64, :],
                                                   self.PS2[0:64, :]))
                            self.sig(s, "s2t")
                        self.ES(bcc3)
                        bct = self.tok("s2t")
                        def yevac3(v, c=c, hof=hof, qb=qb, yof=yof,
                                   yslot=yslot, bct=bct):
                            self.wt(v, "s2t", bct)
                            self.dv(v, "v", v.tensor_mul(
                                YB[hof:hof + 64,
                                   c * T + qb * TB: c * T + (qb + 1) * TB],
                                self.PS4[0:64, yof:yof + 512],
                                self.BCSB[0:64, :]))
                            last_evac_y[yslot] = self.sig(v, "v2t")
                        self.EV(yevac3)
                        last_bcast_use = last_evac_y[yslot]
                        yi += 1

            yb_done = self.tok("v2t")
            oev = {}
            g3 = 0
            for co in range(NCH):
                par = co % 2
                def ldwo3(y, co=co, par=par):
                    self.wt(y, "v2t")
                    self.inc(y.dma_start(
                        out=self.TMP32[:, par * 1024:(par + 1) * 1024]
                        .rearrange("p (dd j) -> p dd j", dd=8),
                        in_=self.w["L1_attn_Wo"][:, co * 128:(co + 1) * 128]
                        .rearrange("(dd p) j -> p dd j", p=128)),
                        "dW" + str(par), 16)
                self.EY(ldwo3)
                dt_ = self.tok("dW" + str(par))
                def cwo3(v, par=par, dt_=dt_):
                    self.wt(v, "dW" + str(par), dt_)
                    self.wt(v, "s2t")
                    self.wt(v, "t2v")
                    self.dv(v, "v", v.tensor_copy(
                        W3T[:, 0:1024],
                        self.TMP32[:, par * 1024:(par + 1) * 1024]))
                    self.sig(v, "v2t")
                self.EV(cwo3)
                wo_tok = self.tok("v2t")
                for tb in range(NTB):
                    gg = g3; g3 += 1
                    def mmo3(t, co=co, tb=tb, gg=gg, wo_tok=wo_tok,
                             yb_done=yb_done):
                        if co == 0 and tb == 0:
                            self.wt(t, "v2t", yb_done)
                        self.wt(t, "v2t", wo_tok)
                        if gg >= 4:
                            self.wt(t, "v2t", oev[gg - 4])
                        for d in range(NCH):
                            self.dv(t, "t", t.matmul(
                                self.slot(gg),
                                W3T[:, d * 128:(d + 1) * 128],
                                YB[:, d * T + tb * TB:
                                   d * T + (tb + 1) * TB],
                                start=(d == 0), stop=False))
                        self.dv(t, "t", t.matmul(
                            self.slot(gg),
                            BROWS[0:1, 3072 + co * 128:
                                  3072 + (co + 1) * 128],
                            C["ones_tb_b"][:, :], start=False, stop=True))
                        self.sig(t, "t2v")
                    self.ET(mmo3)
                    mt = self.tok("t2v")
                    def evo3(v, co=co, tb=tb, gg=gg, mt=mt):
                        self.wt(v, "t2v", mt)
                        xs = self.X32[:, co * T + tb * TB:
                                      co * T + (tb + 1) * TB]
                        self.dv(v, "v", v.scalar_tensor_tensor(
                            xs, self.slot(gg), 0.0, xs, ALU.bypass,
                            ALU.add))
                        oev[gg] = self.sig(v, "v2t")
                    self.EV(evo3)

            self.layernorm("ln3", mirror=True)
            self.fence("tvsgy")


    def p4_l1_ffn(self):
        C = self.CONST
        st = ExitStack()
        with st:
            H16 = self.sb("H16", [128, NFF * TB], BF16, st)
            XNB = self.sb("XNB", [128, NCH * TB], BF16, st)
            WUS = self.sb("WUS", [128, 2048], BF16, st)
            WDS = self.sb("WDS", [128, 2 * 2048], BF16, st)
            H2S = self.sb("H2S", [128, TB], BF16, st)
            MT = self.sb("MT", [128, 2048], BF16, st)
            SGN = self.sb("SGN", [128, 2048], BF16, st)
            WTT = self.sb("WTT", [128, 2048], BF16, st)
            ALW = self.sb("ALW", [128, 1024], F32, st)
            SCO = self.sb("SCO", [128, 1024], F32, st)
            ZB = self.sb("ZB", [128, 1024], F32, st)
            R4B = self.sb("R4B", [1, TB], BF16, st)

            def zb0(v):
                self.dv(v, "v", v.memset(ZB[:, :], 0.0))
                self.sig(v, "v2t")
            self.EV(zb0)
            for wi, nm in enumerate(("L1_ffn_w_up", "L1_ffn_w_down")):
                wap = self.w[nm]
                for t in range(32):
                    def lda(y, wi=wi, t=t, wap=wap):
                        self.wt(y, "s2y")
                        if t == 0:
                            self.wt(y, "v2g")
                            self.wt(y, "v2t")
                        if wi == 0:
                            for pi in range(4):
                                for r in range(32):
                                    row = 32 * (4 * t + pi) + r
                                    self.inc(y.dma_start(
                                        out=ALW[pi * 32:(pi + 1) * 32,
                                                r * 32:r * 32 + 32],
                                        in_=wap[row:row + 1, :].rearrange(
                                            "o (j c) -> (o j) c", j=32)),
                                        "dWa", 16)
                        else:
                            for r in range(32):
                                row = 32 * t + r
                                self.inc(y.dma_start(
                                    out=ALW[:, r * 32:r * 32 + 32],
                                    in_=wap[row:row + 1, :].rearrange(
                                        "o (j c) -> (o j) c", j=128)),
                                    "dWa", 16)
                    self.EY(lda)
                    da_ = self.tok("dWa")
                    def absp(s, t=t, da_=da_, wi=wi):
                        self.wt(s, "dWa", da_)
                        if t > 0 or wi > 0:
                            self.wt(s, "v2s")
                        self.dv(s, "s", s.activation(
                            self.TMP32[:, 0:1024], ALW[:, 0:1024], AF.Abs))
                        self.sig(s, "s2v")
                        self.sig(s, "s2y")
                    self.ES(absp)
                    ab_ = self.tok("s2v")
                    def scanp(v, t=t, wi=wi, ab_=ab_):
                        self.wt(v, "s2v", ab_)
                        if t > 0 or wi > 0:
                            self.wt(v, "dG4s")
                        self.dv(v, "v", v.tensor_tensor_scan(
                            SCO[:, 0:1024], self.TMP32[:, 0:1024],
                            ZB[:, 0:1024], 0.0, ALU.add, ALU.add))
                        self.sig(v, "v2s")
                        self.sig(v, "v2g")
                    self.EV(scanp)
                    sc_ = self.tok("v2g")
                    def stb1(g, t=t, sc_=sc_):
                        self.wt(g, "v2g", sc_)
                        with self.nc.allow_non_contiguous_dma("128x4B col"):
                            self.inc(g.dma_start(
                                out=self.b1d[0:1, t * 128:(t + 1) * 128],
                                in_=SCO[:, 1023:1024]), "dG4s", 16)
                    self.EY(stb1)
                def lds2(g, wi=wi):
                    self.wt(g, "dG4s")
                    self.wt(g, "s2y")
                    if wi == 0:
                        srcap = self.b1d[0:1, :].rearrange(
                            "o (k m) -> (o k) m", k=4)
                        dstap = ALW[0:4, 0:1024]
                    else:
                        srcap = self.b1d[0:1, :].rearrange(
                            "o (i kk c) -> (o kk) i c", i=32, c=32)
                        dstap = ALW[0:4, 0:1024].rearrange(
                            "k (i c) -> k i c", c=32)
                    with self.nc.allow_non_contiguous_dma("alpha stage2"):
                        self.inc(g.dma_start(out=dstap, in_=srcap),
                                 "dG4b", 16)
                self.EY(lds2)
                s2_ = self.tok("dG4b")
                def scan2(v, s2_=s2_):
                    self.wt(v, "dG4b", s2_)
                    self.dv(v, "v", v.tensor_tensor_scan(
                        SCO[0:4, 0:1024], ALW[0:4, 0:1024], ZB[0:4, 0:1024],
                        0.0, ALU.add, ALU.add))
                    self.sig(v, "v2g")
                self.EV(scan2)
                v2_ = self.tok("v2g")
                def stb2(g, v2_=v2_):
                    self.wt(g, "v2g", v2_)
                    with self.nc.allow_non_contiguous_dma("4x4B col"):
                        self.inc(g.dma_start(
                            out=self.b2d[0:1, 0:4],
                            in_=SCO[0:4, 1023:1024]), "dG4c", 16)
                    self.wt(g, "dG4c")
                    self.inc(g.dma_start(out=ALW[0:1, 0:4],
                                         in_=self.b2d[0:1, 0:4]), "dG4d", 16)
                    self.wt(g, "dG4d")
                    self.sig(g, "y2v")
                self.EY(stb2)
                g3_ = self.tok("y2v")
                def scan3(v, wi=wi, g3_=g3_):
                    self.wt(v, "y2v", g3_)
                    self.dv(v, "v", v.tensor_tensor_scan(
                        SCO[0:1, 0:4], ALW[0:1, 0:4], ZB[0:1, 0:4],
                        0.0, ALU.add, ALU.add))
                    self.dv(v, "v", v.tensor_scalar(
                        self.AMX[0:1, 1:2], SCO[0:1, 3:4],
                        1.0 / (1 << 23), None, ALU.mult))
                    self.sig(v, "v2t")
                self.EV(scan3)
                self.repl_col(self.AMX[0:1, 1:2],
                              self.THR[:, wi:wi + 1], "th%d" % wi,
                              exact=True)

            def negthr(v):
                self.dv(v, "v", v.tensor_scalar(
                    self.THR[:, 2:4], self.THR[:, 0:2], -1.0, None,
                    ALU.mult))
                self.sig(v, "v2t")
            self.EV(negthr)
            thr_tok = self.tok("v2t")

            for wi, nm in enumerate(("L1_ffn_w_up", "L1_ffn_w_down")):
                nblk = NFF if wi == 0 else NCH * 2
                for fc in range(nblk):
                    par = fc % 2
                    wide = (wi == 1)
                    def ldt(y, nm=nm, wi=wi, fc=fc, par=par, wide=wide):
                        self.wt(y, "v2s")
                        if wi == 0:
                            src = self.w[nm][fc * 128:(fc + 1) * 128, :]
                            dst = self.TMP32[:, par * 1024:(par + 1) * 1024]
                        else:
                            c, hf = fc // 2, fc % 2
                            src = self.w[nm][c * 128:(c + 1) * 128,
                                             hf * 2048:(hf + 1) * 2048]
                            dst = self.TMP32[:, :]
                        self.inc(y.dma_start(out=dst, in_=src),
                                 "dW" + str(par), 16)
                    self.EY(ldt)
                    dt_ = self.tok("dW" + str(par))
                    ln = 2048 if wide else 1024
                    def msk(v, wi=wi, fc=fc, par=par, dt_=dt_, ln=ln,
                            thr_tok=thr_tok):
                        self.wt(v, "dW" + str(par), dt_)
                        if fc == 0 and wi == 0:
                            self.wt(v, "v2t", thr_tok)
                        self.wt(v, "t2v")
                        self.wt(v, "dG4")
                        src = (self.TMP32[:, :] if ln == 2048 else
                               self.TMP32[:, par * 1024:(par + 1) * 1024])
                        self.dv(v, "v", v.tensor_scalar(
                            MT[:, 0:ln], src, self.THR[:, wi:wi + 1],
                            None, ALU.is_gt))
                        self.dv(v, "v", v.tensor_scalar(
                            SGN[:, 0:ln], src, self.THR[:, 2 + wi:3 + wi],
                            None, ALU.is_lt))
                        self.sig(v, "v2s")
                        self.dv(v, "v", v.tensor_sub(
                            WTT[:, 0:ln], MT[:, 0:ln], SGN[:, 0:ln]))
                        self.sig(v, "v2t")
                    self.EV(msk)
                    wm_ = self.tok("v2t")
                    nrnd = 2 if wide else 1
                    for rnd in range(nrnd):
                        def trt(t, rnd=rnd, wm_=wm_):
                            self.wt(t, "v2t", wm_)
                            self.wt(t, "v2t")
                            for j in range(8):
                                self.dv(t, "t", t.transpose(
                                    self.PSB[:, j * 128:(j + 1) * 128],
                                    WTT[:, rnd * 1024 + j * 128:
                                        rnd * 1024 + (j + 1) * 128],
                                    C["identb"][:, :]))
                            self.sig(t, "t2v")
                        self.ET(trt)
                        tt_ = self.tok("t2v")
                        def evt(v, wi=wi, fc=fc, rnd=rnd, tt_=tt_):
                            self.wt(v, "t2v", tt_)
                            self.wt(v, "dG4")
                            self.dv(v, "v", v.tensor_copy(
                                WTT[:, rnd * 1024:(rnd + 1) * 1024],
                                self.PSB[:, 0:1024]))
                            self.sig(v, "v2g")
                            self.sig(v, "v2t")
                        self.EV(evt)
                        vg_ = self.tok("v2g")
                        def stt_(g_, wi=wi, fc=fc, rnd=rnd, vg_=vg_):
                            self.wt(g_, "v2g", vg_)
                            if wi == 0:
                                dst = self.wutd[fc, :, :]
                            else:
                                c, hf = fc // 2, fc % 2
                                dst = self.wdtd[
                                    c, :, (hf * 2 + rnd) * 1024:
                                    (hf * 2 + rnd + 1) * 1024]
                            self.inc(g_.dma_start(
                                out=dst,
                                in_=WTT[:, rnd * 1024:(rnd + 1) * 1024]),
                                "dG4", 16)
                        self.EY(stt_)

            slot_free = {}
            g4 = 0
            for tb in range(NTB):
                x2tok = {}
                for c in range(NCH):
                    sl = c % 2
                    def x2p(v, c=c, tb=tb, sl=sl):
                        if sl in x2tok:
                            self.wt(v, "t2v", x2tok[sl])
                        src = self.XB[:, c * T + tb * TB:
                                      c * T + (tb + 1) * TB]
                        dst = self.X2S[:, sl * TB:(sl + 1) * TB]
                        self.dv(v, "v", v.tensor_mul(dst, src, src))
                        self.sig(v, "v2t")
                    self.EV(x2p)
                    vt_ = self.tok("v2t")
                    def mmx2(t, c=c, tb=tb, sl=sl, vt_=vt_):
                        self.wt(t, "v2t", vt_)
                        if c == 0:
                            self.wt(t, "s2v")
                        self.dv(t, "t", t.matmul(
                            self.PS4[0:1, 0:512], C["ones_col_b"][:, :],
                            self.X2S[:, sl * TB:(sl + 1) * TB],
                            start=(c == 0), stop=(c == NCH - 1)))
                        x2tok[sl] = self.sig(t, "t2v")
                    self.ET(mmx2)
                def rms1s(s):
                    self.wt(s, "t2v")
                    self.dv(s, "s", s.activation(
                        self.MROW[:, :], self.PS4[0:1, 0:512], AF.Ln,
                        bias=C["epsc"][0:1, 1:2]))
                    self.dv(s, "s", s.activation(
                        self.MROW[:, :], self.MROW[:, :], AF.Exp,
                        scale=0.5))
                    self.sig(s, "s2v")
                self.ES(rms1s)
                r1_ = self.tok("s2v")
                def rms1v(v, r1_=r1_):
                    self.wt(v, "s2v", r1_)
                    self.dv(v, "v", v.tensor_scalar(
                        self.MROW[:, :], self.MROW[:, :],
                        1.0 / 32.0, 1e-8, ALU.mult, ALU.add))
                    with self.nc.allow_low_precision("rms recip bf16"):
                        self.dv(v, "v", v.reciprocal(R4B[:, :],
                                                     self.MROW[:, :]))
                    self.sig(v, "v2t")
                    self.sig(v, "v2s")
                self.EV(rms1v)
                rr_ = self.tok("v2t")
                def bcr1(t, rr_=rr_):
                    self.wt(t, "v2t", rr_)
                    self.wt(t, "v2t")
                    self.dv(t, "t", t.matmul(
                        self.PS2[:, :], C["ones_row_b"][0:1, :],
                        R4B[:, :], start=True, stop=True))
                    self.sig(t, "t2v")
                self.ET(bcr1)
                br_ = self.tok("t2v")
                def xnorm(v, tb=tb, br_=br_):
                    self.wt(v, "t2v", br_)
                    if tb > 0:
                        self.wt(v, "t2y")
                    for c in range(NCH):
                        self.dv(v, "v", v.tensor_mul(
                            XNB[:, c * TB:(c + 1) * TB],
                            self.XB[:, c * T + tb * TB:
                                    c * T + (tb + 1) * TB],
                            self.PS2[:, :]))
                    self.sig(v, "v2t")
                self.EV(xnorm)
                xn_tok = self.tok("v2t")

                for fb in range(NFF):
                    par = fb % 2
                    def ldu(y, tb=tb, fb=fb, par=par):
                        if tb == 0 and fb == 0:
                            self.wt(y, "dG4")
                        if tb > 0 or fb >= 2:
                            self.wt(y, "t2y")
                        self.inc(y.dma_start(
                            out=WUS[:, par * 1024:(par + 1) * 1024],
                            in_=self.wutd[fb, :, :]), "dWu" + str(par), 16)
                    self.EY(ldu)
                    ut_ = self.tok("dWu" + str(par))
                    gg = g4; g4 += 1
                    def mmu(t, tb=tb, fb=fb, par=par, gg=gg, ut_=ut_,
                            xn_tok=xn_tok):
                        if fb == 0:
                            self.wt(t, "v2t", xn_tok)
                        self.wt(t, "dWu" + str(par), ut_)
                        if gg >= 4:
                            self.wt(t, *slot_free[gg - 4])
                        if tb > 0 and fb == 0:
                            self.wt(t, "t2v")
                        for d in range(NCH):
                            self.dv(t, "t", t.matmul(
                                self.slot(gg),
                                WUS[:, par * 1024 + d * 128:
                                    par * 1024 + (d + 1) * 128],
                                XNB[:, d * TB:(d + 1) * TB],
                                start=(d == 0), stop=(d == NCH - 1)))
                        self.sig(t, "t2y")
                        self.sig(t, "t2s")
                    self.ET(mmu)
                    mt = self.tok("t2s")
                    def evu(s, fb=fb, gg=gg, mt=mt):
                        self.wt(s, "t2s", mt)
                        self.dv(s, "s", s.activation(
                            H16[:, fb * TB:(fb + 1) * TB],
                            self.slot(gg), AF.Gelu))
                        slot_free[gg] = ("s2t", self.sig(s, "s2t"))
                        self.sig(s, "s2v")
                    self.ES(evu)
                    hv_ = self.tok("s2v")
                    def h2p(v, fb=fb, hv_=hv_):
                        self.wt(v, "s2v", hv_)
                        if fb > 0:
                            self.wt(v, "t2v")
                        self.dv(v, "v", v.tensor_mul(
                            H2S[:, :], H16[:, fb * TB:(fb + 1) * TB],
                            H16[:, fb * TB:(fb + 1) * TB]))
                        self.sig(v, "v2t")
                    self.EV(h2p)
                    h2_ = self.tok("v2t")
                    def mmh2(t, fb=fb, h2_=h2_):
                        self.wt(t, "v2t", h2_)
                        self.dv(t, "t", t.matmul(
                            self.PS4[0:1, 512:1024], C["ones_col_b"][:, :],
                            H2S[:, :], start=(fb == 0),
                            stop=(fb == NFF - 1)))
                        self.sig(t, "t2v")
                    self.ET(mmh2)

                def rms2s(s):
                    self.wt(s, "t2v")
                    self.dv(s, "s", s.activation(
                        self.AROW[:, :], self.PS4[0:1, 512:1024], AF.Ln,
                        bias=C["epsc"][0:1, 1:2]))
                    self.dv(s, "s", s.activation(
                        self.AROW[:, :], self.AROW[:, :], AF.Exp,
                        scale=0.5))
                    self.sig(s, "s2v")
                self.ES(rms2s)
                r2_ = self.tok("s2v")
                def rms2v(v, r2_=r2_):
                    self.wt(v, "s2v", r2_)
                    self.dv(v, "v", v.tensor_scalar(
                        self.AROW[:, :], self.AROW[:, :],
                        1.0 / 64.0, 1e-8, ALU.mult, ALU.add))
                    self.dv(v, "v", v.reciprocal(self.BROW[:, :],
                                                 self.AROW[:, :]))
                    with self.nc.allow_low_precision("r2 hi/lo"):
                        self.dv(v, "v", v.tensor_copy(
                            self.HLROW[0:1, 0:512], self.BROW[:, :]))
                        self.dv(v, "v", v.tensor_sub(
                            self.TMP32[0:1, 0:512], self.BROW[:, :],
                            self.HLROW[0:1, 0:512]))
                        self.dv(v, "v", v.tensor_copy(
                            self.HLROW[0:1, 512:1024],
                            self.TMP32[0:1, 0:512]))
                    self.sig(v, "v2t")
                self.EV(rms2v)
                rr2_ = self.tok("v2t")
                def bcr2(t, rr2_=rr2_):
                    self.wt(t, "v2t", rr2_)
                    self.wt(t, "v2t")
                    self.dv(t, "t", t.matmul(
                        self.PS2[:, :], C["ones_row_b"][0:1, :],
                        self.HLROW[0:1, 0:512], start=True, stop=False))
                    self.dv(t, "t", t.matmul(
                        self.PS2[:, :], C["ones_row_b"][0:1, :],
                        self.HLROW[0:1, 512:1024], start=False, stop=True))
                    self.sig(t, "t2v")
                self.ET(bcr2)
                br2t = self.tok("t2v")
                def bcc4(s, br2t=br2t):
                    s.wait_ge(self.sem("t2v"), br2t)
                    self.dv(s, "s", s.copy(self.BCSB[:, :], self.PS2[:, :]))
                    self.sig(s, "s2t")
                self.ES(bcc4)
                br2_ = self.tok("s2t")

                for co in range(NCH):
                    for hf in range(2):
                        def ldd(y, tb=tb, co=co, hf=hf):
                            if tb > 0 or co >= 1:
                                self.wt(y, "t2y")
                            self.inc(y.dma_start(
                                out=WDS[:, hf * 2048:(hf + 1) * 2048],
                                in_=self.wdtd[co, :, hf * 2048:
                                              (hf + 1) * 2048]),
                                "dWd" + str(hf), 16)
                        self.EY(ldd)
                    dt0 = self.tok("dWd0")
                    dt1 = self.tok("dWd1")
                    gg = g4; g4 += 1
                    def mmd(t, tb=tb, co=co, gg=gg, dt0=dt0, dt1=dt1,
                            br2_=br2_):
                        if co == 0:
                            self.wt(t, "s2t", br2_)
                            self.wt(t, "s2t")
                        self.wt(t, "dWd0", dt0)
                        self.wt(t, "dWd1", dt1)
                        if gg >= 4:
                            self.wt(t, *slot_free[gg - 4])
                        for fb in range(NFF):
                            self.dv(t, "t", t.matmul(
                                self.slot(gg),
                                WDS[:, fb * 128:(fb + 1) * 128],
                                H16[:, fb * TB:(fb + 1) * TB],
                                start=(fb == 0), stop=(fb == NFF - 1)))
                        self.sig(t, "t2y")
                        self.sig(t, "t2v")
                    self.ET(mmd)
                    mt = self.tok("t2v")
                    def evd(v, tb=tb, co=co, gg=gg, mt=mt):
                        self.wt(v, "t2v", mt)
                        xs = self.X32[:, co * T + tb * TB:
                                      co * T + (tb + 1) * TB]
                        self.dv(v, "v", v.tensor_mul(
                            self.TMP32[:, 0:512], self.slot(gg),
                            self.BCSB[:, :]))
                        self.dv(v, "v", v.tensor_add(
                            xs, xs, self.TMP32[:, 0:512]))
                        slot_free[gg] = ("v2t", self.sig(v, "v2t"))
                    self.EV(evd)

            self.layernorm("ln4", mirror=False)
            self.fence("tvsgy")

    def dbg_put(self, src_ap, col, n=1):
        def f(v, src_ap=src_ap, col=col, n=n):
            self.dv(v, "v", v.tensor_copy(
                self.DBGROW[0:1, col:col + n], src_ap))
            self.sig(v, "v2y")
        self.EV(f)

    def p5_store(self):
        self.fence("tvsgy")
        def dstore(y):
            self.inc(y.dma_start(out=self.dbg[:, :],
                                 in_=self.DBGROW[:, :]), "dOut0", 16)
        self.EY(dstore)
        for c in range(NCH):
            def st(y, c=c):
                self.inc(y.dma_start(
                    out=self.out[c * 128:(c + 1) * 128, :],
                    in_=self.X32[:, c * T:(c + 1) * T]), "dOut0", 16)
            self.EY(st)

        def fin(y):
            self.wt(y, "dOut0")
        self.EY(fin)


_NC_CACHE = {}


def _build(race_dev=False, stop_after=None):
    key = (race_dev, stop_after)
    if key not in _NC_CACHE:
        mk = MK(race_dev=race_dev, stop_after=stop_after)
        _NC_CACHE[key] = mk.build()
    return _NC_CACHE[key]


_HOST_T = ("L0_ffn_W_in", "L1_attn_Wq", "L1_attn_Wk", "L1_attn_Wv",
           "L1_attn_Wo", "L1_attn_Wg1", "L1_attn_Wg2")


def kernel(**inputs):
    x = np.asarray(inputs["L0_attn_x"], np.float32)
    nc = _build()
    wmap = {}
    for k, v in inputs.items():
        if k == "L0_attn_x":
            continue
        a = np.asarray(v, np.float32)
        if k in _HOST_T:
            a = a.T
        wmap[k] = np.ascontiguousarray(a)
    in_maps = [dict(wmap, x=np.ascontiguousarray(x[b].T))
               for b in range(NCORES)]
    res = run_bass_kernel_spmd(nc, in_maps, core_ids=list(range(NCORES)))
    return np.ascontiguousarray(
        np.stack([r["out"].T for r in res.results], axis=0))
